# version 30
# baseline (speedup 1.0000x reference)
import os
import numpy as np
import ml_dtypes
from contextlib import ExitStack

import concourse.bass as bass
import concourse.mybir as mybir
from concourse.bass_utils import run_bass_kernel_spmd

F32 = mybir.dt.float32
BF16 = mybir.dt.bfloat16
AF = mybir.ActivationFunctionType
ALU = mybir.AluOpType
BF = ml_dtypes.bfloat16

D = 1024
NH, HD = 16, 64
TO = 2048
HW = 128
TH = TO + 2 * HW
NT = TH // 128
NQ = TO // 128
EPS = 1e-5
KNQL = int(os.environ.get("KNQL", "16"))
KPHA = int(os.environ.get("KPHA", "1"))
KSEC = int(os.environ.get("KSEC", "8"))
SPANS = [(3 * i, 3) for i in range(6)]
NSP = len(SPANS)
TILES = [(kb, hg) for kb in (0, 2, 1) for hg in range(4)]


class Sem:
    def __init__(self, ctx, nc, name, ev):
        self.h = ctx.enter_context(nc.semaphore(name))
        self.n = 0
        self.name = name
        self.ev = ev

    def inc(self, inst, key=None, k=1):
        self.n += k
        if inst is not None and k > 0:
            inst.then_inc(self.h, k)
        if key is not None:
            self.ev[(self.name,) + tuple(key)] = self.n
        return self.n

    def wait(self, eng, key):
        t = self.ev.get((self.name,) + tuple(key), 0)
        if t > 0:
            eng.wait_ge(self.h, t)

    def selfwait(self, eng):
        eng.wait_ge(self.h, self.n)

    def edge(self, inst, eng):
        self.n += 1
        inst.then_inc(self.h, 1)
        eng.wait_ge(self.h, self.n)


def q_range(t0, nt):
    return max(t0 - 1, 0), min(t0 + nt - 1, NQ)


def q_cols(si):
    t0, nt = SPANS[si]
    qlo, qhi = q_range(t0, nt)
    return qlo, (qlo + 1 - t0) * 128, (qhi - qlo) * 128


def build_nc(ev):
    nc = bass.Bass("TRN2", target_bir_lowering=False)

    d_xh = nc.dram_tensor("xh", [TH, D], F32, kind="ExternalInput").ap()
    d_wqkv = nc.dram_tensor("wqkv", [D, 3 * D], F32, kind="ExternalInput").ap()
    d_wout = nc.dram_tensor("wout", [D, D], F32, kind="ExternalInput").ap()
    d_bout = nc.dram_tensor("bout", [D], F32, kind="ExternalInput").ap()
    d_g1 = [nc.dram_tensor("g1r", [1, 32, 4, 32], F32, kind="ExternalInput").ap(),
            nc.dram_tensor("g1i", [1, 32, 4, 32], F32, kind="ExternalInput").ap()]
    d_g2 = [nc.dram_tensor("g2r", [4, 16, 1, 16], F32, kind="ExternalInput").ap(),
            nc.dram_tensor("g2i", [4, 16, 1, 16], F32, kind="ExternalInput").ap()]
    d_br = nc.dram_tensor("br", [512], F32, kind="ExternalInput").ap()
    d_bi = nc.dram_tensor("bi", [512], F32, kind="ExternalInput").ap()
    d_idn = nc.dram_tensor("idn", [128, 128], BF16, kind="ExternalInput").ap()
    d_prot = nc.dram_tensor("prot", [128, 128], BF16, kind="ExternalInput").ap()
    d_crep = nc.dram_tensor("crep", [128, TH], BF16, kind="ExternalInput").ap()
    d_srep = nc.dram_tensor("srep", [128, TH], BF16, kind="ExternalInput").ap()
    d_msk = nc.dram_tensor("msk", [NQ, 2, 128, 128], BF16, kind="ExternalInput").ap()
    d_y = nc.dram_tensor("y", [TO, D], F32, kind="ExternalOutput").ap()
    d_scr = nc.dram_tensor("wscr", [2, 1024, 256], BF16, kind="Internal").ap()

    ctx = ExitStack()
    sbr = lambda n, s, d: ctx.enter_context(nc.sbuf_tensor(n, s, d))

    q_fm = sbr("q_fm", [128, 8 * TO], BF16)
    k_fm = sbr("k_fm", [128, 8 * TH], BF16)
    v_tm = sbr("v_tm", [128, NT * D], BF16)
    wffn = sbr("wffn", [128, 12 * 512], BF16)
    idn_sb = sbr("idn_sb", [128, 128], BF16)
    prot_sb = sbr("prot_sb", [128, 128], BF16)
    ones_k = sbr("ones_k", [128, 1], BF16)
    ones_r = sbr("ones_r", [1, 128], BF16)
    msk_sb = sbr("msk_sb", [128, 2 * 256], BF16)
    bn6 = sbr("bn6", [128, 2 * 12], F32)
    bnag = sbr("bnag", [128, 2 * 2], F32)
    lnv = sbr("lnv", [128, 1], F32)
    rstd = sbr("rstd", [128, 2], F32)
    eps_sb = sbr("eps_sb", [128, 1], F32)
    p_wb = ctx.enter_context(nc.psum_tensor("p_wb", [128, 512], F32))

    ctxA = ExitStack()
    sba = lambda n, s, d: ctxA.enter_context(nc.sbuf_tensor(n, s, d))
    psa = lambda n, s, d: ctxA.enter_context(nc.psum_tensor(n, s, d))
    wqkv_sb = sba("wqkv_sb", [128, 8 * 3072], BF16)
    s_g1 = [sba("s_g1r", [4, 1024], BF16), sba("s_g1i", [4, 1024], BF16)]
    s_g2 = [sba("s_g2r", [4, 256], BF16), sba("s_g2i", [4, 256], BF16)]
    s_g1db = sba("s_g1db", [4, 1024], BF16)
    s_g2ge = sba("s_g2ge", [4, 256], BF16)
    s_pw = sba("s_pw", [128, 2 * 256], BF16)
    crepS = sba("crepS", [128, 384], BF16)
    srepS = sba("srepS", [128, 384], BF16)
    xfm = sba("xfm", [128, 2 * 8 * 384], BF16)
    xa = sba("xa", [128, 2 * 1024], F32)
    xln = sba("xln", [128, 1024], BF16)
    m1b = sba("m1b", [128, 4 * 384], BF16)
    m2b = sba("m2b", [128, 4 * 384], BF16)
    pq_sb = sba("pq_sb", [128, 4 * 384], BF16)
    p_tp = [psa("p_tp0", [128, 1024], BF16)]
    p_qk = [psa("p_qk0", [128, 512], F32), psa("p_qk1", [128, 512], F32)]
    p_v = [psa("p_v0", [128, 512], F32), psa("p_v1", [128, 512], F32)]
    p_pq = [psa("p_pq0", [128, 512], F32), psa("p_pq1", [128, 512], F32)]
    ctxA.close()

    ctxB = ExitStack()
    sbb = lambda n, s, d: ctxB.enter_context(nc.sbuf_tensor(n, s, d))
    psb = lambda n, s, d: ctxB.enter_context(nc.psum_tensor(n, s, d))
    wout_sb = sbb("wout_sb", [128, 8 * 1024], BF16)
    bias_sb = sbb("bias_sb", [1, 2048], BF16)
    e_sb = sbb("e_sb", [128, 2 * 3 * 2048], BF16)
    xb = sbb("xb", [128, 2 * 1024], F32)
    y1 = sbb("y1", [128, 2 * 1024], F32)
    h2 = sbb("h2", [128, 1024], BF16)
    h2fm = sbb("h2fm", [128, 8 * 128], BF16)
    attn_fm = sbb("attn_fm", [128, 1024], BF16)
    rep_sb = sbb("rep_sb", [128, 1024], BF16)
    r_row = sbb("r_row", [1, 2048], BF16)
    rtmp33 = sbb("rtmp33", [33, 1024], F32)
    q_z = sbb("q_z", [128, 2 * 2048], BF16)
    xs = sbb("xs", [128, 1024], F32)
    bnagB = sbb("bnagB", [128, 3 * 2 * 2], F32)
    rstdB = sbb("rstdB", [128, 3 * 2], F32)
    lnvB = sbb("lnvB", [128, 1], F32)
    negmr = sbb("negmr", [128, 1], F32)
    p_scA = psb("p_scA", [128, 1024], F32)
    p_scB = psb("p_scB", [128, 1024], F32)
    p_att = psb("p_att", [128, 1024], F32)
    p_smA = psb("p_smA", [128, 512], F32)
    ctxB.close()

    p_tp2 = p_smA[:].bitcast(BF16)

    S = lambda n: Sem(ctx, nc, n, ev)
    sW = S("sW"); sCC = S("sCC")
    sCS = [S("sCS0"), S("sCS1")]
    sXp = [S("sX0"), S("sX1")]
    sBN = S("sBN"); sRS = S("sRS")
    sLN = S("sLN"); sTP = S("sTP"); sFM = S("sFM"); sQG = S("sQG")
    sQE = S("sQE"); sPQ = S("sPQ"); sPQE = S("sPQE"); sRP = S("sRP"); sVE = S("sVE")
    sAB = S("sAB"); sWM = S("sWM"); sWE = S("sWE")
    sWSp = [S("sWS0"), S("sWS1")]; sWL = S("sWL")
    sG = [S("sG0"), S("sG1")]; sBIA = S("sBIA"); sWO = S("sWO")
    sXBp = [S("sXB0"), S("sXB1")]; sMKp = [S("sMK0"), S("sMK1")]
    sQZp = [S("sQZ0"), S("sQZ1")]
    sSC = S("sSC"); sE = S("sE"); sEM = S("sEM")
    sSM = S("sSM"); sRC = S("sRC"); sRCln = S("sRCln"); sDR = S("sDR")
    sPA = S("sPA"); sAM = S("sAM"); sPJ = S("sPJ"); sY1 = S("sY1")
    sH2 = S("sH2"); sT2 = S("sT2"); sF2 = S("sF2"); sFF = S("sFF")
    sYO = S("sYO"); sOUTp = [S("sOUT0"), S("sOUT1")]
    sv = S("sv"); sa = S("sa"); svW = S("svW")

    def rope_slice(sp, fc):
        t0, nt = SPANS[sp]
        qlo, qloc, qn = q_cols(sp)
        if fc < 8:
            return (q_fm[:, fc * TO + qlo * 128:fc * TO + qlo * 128 + qn], qloc, qn)
        kc = fc - 8
        return (k_fm[:, kc * TH + t0 * 128:kc * TH + (t0 + nt) * 128], 0, nt * 128)

    def em_sync(eng):
        i = eng.dma_start(out=idn_sb[:], in_=d_idn)
        sCC.inc(i, None, 16)
        i = eng.dma_start(out=prot_sb[:], in_=d_prot)
        sCC.inc(i, ("cc",), 16)
        for si, (t0, nt) in enumerate(SPANS):
            for t in range(t0, t0 + nt):
                sLN.wait(eng, (t - 2,))
                i = eng.dma_start(out=xa[:, (t % 2) * 1024:(t % 2) * 1024 + 1024],
                                  in_=d_xh[t * 128:(t + 1) * 128, :])
                sXp[t % 2].inc(i, (t,), 16)
            sRP.wait(eng, (si - 1, 3))
            i = eng.dma_start(out=crepS[:],
                              in_=d_crep[:, t0 * 128:t0 * 128 + 384])
            sCS[si % 2].inc(i, None, 16)
            i = eng.dma_start(out=srepS[:],
                              in_=d_srep[:, t0 * 128:t0 * 128 + 384])
            sCS[si % 2].inc(i, ("cs", si), 16)
        eng.wait_ge(sAB.h, 3)
        for qb in range(KNQL + 2):
            if qb < KNQL:
                sSC.wait(eng, (qb - 2, 1, 3))
                if qb < 2:
                    sv.wait(eng, ("qz",))
                qcols = q_fm[:].rearrange("p (c t) -> p c t", c=8)[:, :, qb * 128:(qb + 1) * 128]
                zslot = (qb % 2) * 2048
                qzv = q_z[:, zslot:zslot + 2048].rearrange(
                    "p (c two t) -> p c two t", c=8, two=2)
                i = eng.dma_start(out=qzv[0:64, :, 0, :], in_=qcols[0:64])
                sQZp[qb % 2].inc(i, None, 16)
                i = eng.dma_start(out=qzv[64:128, :, 1, :], in_=qcols[64:128])
                sQZp[qb % 2].inc(i, (qb,), 16)
                sEM.wait(eng, (qb - 2, 2))
                i = eng.dma_start(
                    out=msk_sb[:, (qb % 2) * 256:(qb % 2) * 256 + 256]
                        .rearrange("p (mi f) -> p mi f", mi=2),
                    in_=d_msk[qb].rearrange("mi p f -> p mi f"))
                sMKp[qb % 2].inc(i, (qb,), 16)
                sY1.wait(eng, (qb - 2,))
                i = eng.dma_start(out=xb[:, (qb % 2) * 1024:(qb % 2) * 1024 + 1024],
                                  in_=d_xh[HW + qb * 128:HW + (qb + 1) * 128, :])
                sXBp[qb % 2].inc(i, (qb,), 16)

    def em_gpsimd(eng):
        for m in range(2):
            i = eng.dma_start(out=s_g1[m][:].rearrange("c (b d) -> c b d", b=32),
                              in_=d_g1[m][0].rearrange("b c d -> c b d"))
            sG[m].inc(i, None, 16)
            i = eng.dma_start(out=s_g2[m][:],
                              in_=d_g2[m][:, :, 0, :].rearrange("c e g -> c (e g)"))
            sG[m].inc(i, ("g", m), 16)
        for t3 in range(3):
            for dc in range(8):
                i = eng.dma_start(
                    out=wqkv_sb[:, dc * 3072 + t3 * 1024:
                                   dc * 3072 + (t3 + 1) * 1024],
                    in_=d_wqkv[dc * 128:(dc + 1) * 128,
                               t3 * 1024:(t3 + 1) * 1024])
                sW.inc(i, ("t3", t3) if dc == 7 else None, 16)
        for m in range(2):
            for mt in range(8):
                r = 8 * m + mt
                sWE.wait(eng, (m, mt))
                i = eng.dma_start(out=d_scr[m, mt * 128:(mt + 1) * 128, :],
                                  in_=s_pw[:, (r % 2) * 256:(r % 2) * 256 + 256])
                sWSp[r % 2].inc(i, (r,), 16)
            sWSp[0].wait(eng, (8 * m + 6,))
            sWSp[1].wait(eng, (8 * m + 7,))
            scr_r = d_scr[m].rearrange("(d b) (g e) -> d g b e", d=32, b=32, g=16)
            base = 0 if m == 0 else 4
            for ci in range(4):
                for dr in range(8):
                    i = eng.dma_start(
                        out=wffn[dr * 16:(dr + 1) * 16,
                                 (base + ci) * 512:(base + ci + 1) * 512],
                        in_=scr_r[ci * 8 + dr])
                    sWL.inc(i, None, 16)
            sWL.inc(None, (m,), 0)
        eng.wait_ge(sAB.h, 3)
        i = eng.dma_start(out=bias_sb[0:1, 0:1024], in_=d_bout.unsqueeze(0))
        sBIA.inc(i, None, 16)
        i = eng.dma_start(out=bias_sb[0:1, 1024:1536], in_=d_br.unsqueeze(0))
        sBIA.inc(i, None, 16)
        i = eng.dma_start(out=bias_sb[0:1, 1536:2048], in_=d_bi.unsqueeze(0))
        sBIA.inc(i, ("bias",), 16)
        for dc in range(8):
            i = eng.dma_start(out=wout_sb[:, dc * 1024:(dc + 1) * 1024],
                              in_=d_wout[dc * 128:(dc + 1) * 128, :])
            sWO.inc(i, ("wout",) if dc == 7 else None, 16)
        for qb in range(KNQL + 2):
            so = qb - 2
            if qb < KNQL:
                esl = (qb % 2) * 6144
                for hg in range(4):
                    sE.wait(eng, (qb, 0, hg))
                sMKp[qb % 2].wait(eng, (qb,))
                mk0 = msk_sb[:, (qb % 2) * 256:(qb % 2) * 256 + 128]
                ev0 = e_sb[:, esl:esl + 2048].rearrange("p (h f) -> p h f", h=16)
                i = eng.tensor_tensor(
                    ev0, ev0, mk0.unsqueeze(1).broadcast_to([128, 16, 128]),
                    ALU.mult)
                sEM.inc(i, (qb, 0))
                for hg in range(4):
                    sE.wait(eng, (qb, 2, hg))
                mk2 = msk_sb[:, (qb % 2) * 256 + 128:(qb % 2) * 256 + 256]
                ev2 = e_sb[:, esl + 4096:esl + 6144].rearrange("p (h f) -> p h f", h=16)
                i = eng.tensor_tensor(
                    ev2, ev2, mk2.unsqueeze(1).broadcast_to([128, 16, 128]),
                    ALU.mult)
                sEM.inc(i, (qb, 2))
            if 0 <= so < KNQL:
                sYO.wait(eng, (so,))
                i = eng.dma_start(out=d_y[so * 128:(so + 1) * 128, :], in_=xs[:])
                sOUTp[so % 2].inc(i, (so,), 16)
        for sp_ in sOUTp:
            if sp_.n > 0:
                eng.wait_ge(sp_.h, sp_.n)

    def em_tensor(eng):
        wrounds = [(m, mt) for m in range(2) for mt in range(8)]
        WSL = [(0, 3), (3, 6), (6, 8), (8, 11), (11, 14), (14, 16), (16, 16)]

        def _wmm(b):
            for m, mt in wrounds[WSL[b][0]:WSL[b][1]]:
                if mt == 0:
                    svW.wait(eng, ("sh", m))
                if mt >= 1:
                    sWE.wait(eng, (m, mt - 1))
                elif m == 1:
                    sWE.wait(eng, (0, 7))
                i = eng.matmul(p_wb[:, 0:256], s_g1db[:, mt * 128:(mt + 1) * 128],
                               s_g2ge[:], start=True, stop=True)
                sWM.inc(i, (m, mt))
        sCC.wait(eng, ("cc",))
        vix = 0
        for b in range(NSP + 1):
            _wmm(b)
            if b < NSP:
                si = b
                t0, nt = SPANS[si]
                for t in range(t0, t0 + nt):
                    sLN.wait(eng, (t,))
                    sFM.wait(eng, (t - 1,))
                    for c in range(8):
                        i = eng.transpose(p_tp[0][:, c * 128:(c + 1) * 128],
                                          xln[:, c * 128:(c + 1) * 128], idn_sb[:])
                    sTP.inc(i, (t,))
                sFM.wait(eng, (t0 + nt - 1,))
                qlo, qloc, qn = q_cols(si)
                for fc in range(16):
                    if si == 0 and fc in (0, 8):
                        sW.wait(eng, ("t3", fc // 8))
                    bank = p_qk[fc % 2]
                    if fc >= 2:
                        sQE.wait(eng, (si, fc - 2))
                    else:
                        sQE.wait(eng, (si - 1, 14 + fc))
                    off, n = (qloc, qn) if fc < 8 else (0, nt * 128)
                    for dc in range(8):
                        i = eng.matmul(
                            bank[:, 0:n],
                            wqkv_sb[:, dc * 3072 + fc * 128:dc * 3072 + (fc + 1) * 128],
                            xfm[:, (si % 2) * 3072 + dc * 384 + off:
                                   (si % 2) * 3072 + dc * 384 + off + n],
                            start=(dc == 0), stop=(dc == 7))
                    sQG.inc(i, (si, fc))
                    if si >= 1:
                        sp = si - 1
                        raw, _, pn = rope_slice(sp, fc)
                        sQE.wait(eng, (sp, fc))
                        sPQE.wait(eng, (sp, fc - 2))
                        if fc < 2:
                            sPQE.wait(eng, (sp - 1, 14 + fc))
                        i = eng.matmul(p_pq[fc % 2][:, 0:pn], prot_sb[:], raw,
                                       start=True, stop=True)
                        sPQ.inc(i, (sp, fc))
                for t in range(t0, t0 + nt):
                    for half in range(2):
                        if si == 0 and t == t0 and half == 0:
                            sW.wait(eng, ("t3", 2))
                        bank = p_v[vix % 2]
                        sVE.wait(eng, (vix - 2,))
                        for dc in range(8):
                            i = eng.matmul(
                                bank[:],
                                xfm[:, (si % 2) * 3072 + dc * 384 + (t - t0) * 128:
                                       (si % 2) * 3072 + dc * 384 + (t - t0) * 128 + 128],
                                wqkv_sb[:, dc * 3072 + 2048 + half * 512:
                                           dc * 3072 + 2048 + (half + 1) * 512],
                                start=(dc == 0), stop=(dc == 7))
                        sQG.inc(i, (si, "v", t, half))
                        vix += 1
                sQG.inc(None, (si, "vend"), 0)
            if b == NSP:
                sp = NSP - 1
                for fc in range(16):
                    raw, _, pn = rope_slice(sp, fc)
                    sQE.wait(eng, (sp, fc))
                    sPQE.wait(eng, (sp, fc - 2))
                    if fc < 2:
                        sPQE.wait(eng, (sp - 1, 14 + fc))
                    i = eng.matmul(p_pq[fc % 2][:, 0:pn], prot_sb[:], raw,
                                   start=True, stop=True)
                    sPQ.inc(i, (sp, fc))
        i = eng.nop()
        sAB.inc(i, ("t",))
        eng.wait_ge(sAB.h, 3)
        for qb in range(KNQL + 2):
            po = qb - 1
            so = qb - 2

            def _sc(lo, hi):
                if qb >= KNQL:
                    return
                if qb < 2 and lo == 0:
                    sv.wait(eng, ("qz",))
                for ti in range(lo, hi):
                    kb, hg = TILES[ti]
                    bank = p_scA[:, (ti % 2) * 512:(ti % 2) * 512 + 512]
                    if ti >= 2:
                        pkb, phg = TILES[ti - 2]
                        sE.wait(eng, (qb, pkb, phg))
                    else:
                        pkb, phg = TILES[10 + ti]
                        sE.wait(eng, (qb - 1, pkb, phg))
                        sRCln.wait(eng, (qb - 1,))
                    if ti == 0:
                        sQZp[qb % 2].wait(eng, (qb,))
                    for cc in range(2):
                        c = hg * 2 + cc
                        i = eng.matmul(
                            bank[:, cc * 256:(cc + 1) * 256],
                            k_fm[:, c * TH + (qb + kb) * 128:
                                    c * TH + (qb + kb) * 128 + 128],
                            q_z[:, (qb % 2) * 2048 + c * 256:
                                   (qb % 2) * 2048 + (c + 1) * 256],
                            start=True, stop=True)
                    sSC.inc(i, (qb, kb, hg))

            if 0 <= po < KNQL:
                sEM.wait(eng, (po, 0))
                sEM.wait(eng, (po, 2))
                for hg in range(4):
                    sE.wait(eng, (po, 1, hg))
                sAM.wait(eng, (po - 1,))
                esl = (po % 2) * 6144
                for c in range(8):
                    for par in range(2):
                        h = 2 * c + par
                        for kb in range(3):
                            i = eng.matmul(
                                p_att[par * 64:(par + 1) * 64, c * 128:(c + 1) * 128],
                                v_tm[:, (po + kb) * 1024 + h * 64:
                                        (po + kb) * 1024 + (h + 1) * 64],
                                e_sb[:, esl + kb * 2048 + h * 128:
                                        esl + kb * 2048 + (h + 1) * 128],
                                start=(kb == 0), stop=(kb == 2),
                                tile_position=(0, 64) if par else None)
                sPA.inc(i, (po,))
            _sc(0, 2)
            if 0 <= po < KNQL:
                sRC.wait(eng, (po,))
                sY1.wait(eng, (po - 1,))
                for c in range(8):
                    for par in range(2):
                        h = 2 * c + par
                        i = eng.matmul(
                            p_scB[par * 64:(par + 1) * 64, c * 128:(c + 1) * 128],
                            ones_r[0:1, 0:64],
                            r_row[0:1, h * 128:(h + 1) * 128],
                            start=True, stop=True,
                            tile_position=(0, 64) if par else None)
                sDR.inc(i, (po,))
            _sc(2, 6)
            if 0 <= po < KNQL:
                sAM.wait(eng, (po,))
                sY1.wait(eng, (po - 1,))
                if po == 0:
                    sWO.wait(eng, ("wout",))
                    sBIA.wait(eng, ("bias",))
                    sWE.wait(eng, (1, 7))
                for half in range(2):
                    for c in range(8):
                        eng.matmul(
                            p_scB[:, half * 512:(half + 1) * 512],
                            attn_fm[:, c * 128:(c + 1) * 128],
                            wout_sb[:, c * 1024 + half * 512:c * 1024 + (half + 1) * 512],
                            start=(c == 0), stop=False)
                    i = eng.matmul(p_scB[:, half * 512:(half + 1) * 512],
                                   ones_r[0:1, :],
                                   bias_sb[0:1, half * 512:(half + 1) * 512],
                                   start=False, stop=True)
                sPJ.inc(i, (po,))
            _sc(6, 10)
            if 0 <= so < KNQL:
                sH2.wait(eng, (so,))
                sYO.wait(eng, (so - 1,))
                for c in range(8):
                    i = eng.transpose(p_tp2[:, c * 128:(c + 1) * 128],
                                      h2[:, c * 128:(c + 1) * 128],
                                      idn_sb[:])
                sT2.inc(i, (so,))
            _sc(10, 12)
            if 0 <= so < KNQL:
                sF2.wait(eng, (so,))
                if so == 0:
                    svW.wait(eng, ("neg",))
                for half in range(2):
                    tgt = p_smA[:] if half == 0 else p_wb[:]
                    slots = ([0, 8] if half == 0 else [4, 0])
                    for ci in range(4):
                        eng.matmul(
                            tgt,
                            h2fm[:, ci * 128:(ci + 1) * 128],
                            wffn[:, (slots[0] + ci) * 512:(slots[0] + ci + 1) * 512],
                            start=(ci == 0), stop=False)
                    for ci in range(4):
                        eng.matmul(
                            tgt,
                            h2fm[:, (4 + ci) * 128:(4 + ci + 1) * 128],
                            wffn[:, (slots[1] + ci) * 512:(slots[1] + ci + 1) * 512],
                            start=False, stop=False)
                    i = eng.matmul(tgt,
                                   ones_r[0:1, :],
                                   bias_sb[0:1, 1024 + half * 512:1024 + (half + 1) * 512],
                                   start=False, stop=True)
                sFF.inc(i, (so,))
            if qb < KNQL:
                sE.wait(eng, (qb, TILES[10][0], TILES[10][1]))
                sE.wait(eng, (qb, TILES[11][0], TILES[11][1]))
                sEM.wait(eng, (qb, 0))
                sEM.wait(eng, (qb, 2))
                for hg in range(4):
                    sE.wait(eng, (qb, 1, hg))
                esl2 = (qb % 2) * 6144
                for g in range(4):
                    pg = (g % 2) * 32
                    for kb in range(3):
                        i = eng.matmul(
                            p_scA[pg:pg + 1, (g // 2) * 512:(g // 2) * 512 + 512],
                            ones_k[:],
                            e_sb[:, esl2 + kb * 2048 + g * 512:
                                    esl2 + kb * 2048 + (g + 1) * 512],
                            start=(kb == 0), stop=(kb == 2))
                sSM.inc(i, (qb,))

    def em_scalar(eng):
        sv.wait(eng, ("init",))
        wrounds = [(m, mt) for m in range(2) for mt in range(8)]
        WSL = [(0, 3), (3, 6), (6, 8), (8, 11), (11, 14), (14, 16), (16, 16)]

        def _wevac(b):
            for m, mt in wrounds[WSL[b][0]:WSL[b][1]]:
                r = 8 * m + mt
                sWM.wait(eng, (m, mt))
                if r >= 2:
                    sWSp[r % 2].wait(eng, (r - 2,))
                i = eng.copy(s_pw[:, (r % 2) * 256:(r % 2) * 256 + 256],
                             p_wb[:, 0:256])
                sWE.inc(i, (m, mt))
        def _rstd(b, t):
            sLN.wait(eng, (t - 2,))
            sBN.wait(eng, (t,))
            sl = t % 2
            i = eng.activation(lnv[:], bnag[:, 2 * sl + 1:2 * sl + 2],
                               AF.Ln, bias=eps_sb[:], scale=1.0)
            sa.edge(i, eng)
            i = eng.activation(rstd[:, sl:sl + 1], lnv[:], AF.Exp,
                               bias=0.0, scale=-0.5)
            sRS.inc(i, (t,))
        for b in range(NSP + 1):
            if b < NSP:
                t0, nt = SPANS[b]
                _rstd(b, t0)
                _rstd(b, t0 + 1)
            _wevac(b)
            if b < NSP:
                _rstd(b, t0 + 2)
                for t in range(t0, t0 + nt):
                    sTP.wait(eng, (t,))
                    sQG.wait(eng, (b - 2, "vend"))
                    i = eng.copy(
                        xfm[:, (b % 2) * 3072:(b % 2) * 3072 + 3072]
                            .rearrange("p (c f) -> p c f", c=8)
                            [:, :, (t - t0) * 128:(t - t0) * 128 + 128],
                        p_tp[0][:].rearrange("p (c f) -> p c f", c=8))
                    sFM.inc(i, (t,))
            for fc in range(16):
                if b < NSP:
                    raw, _, n = rope_slice(b, fc)
                    sQG.wait(eng, (b, fc))
                    i = eng.copy(raw, p_qk[fc % 2][:, 0:n])
                    sQE.inc(i, (b, fc))
                if b >= 1:
                    sp = b - 1
                    _, _, n = rope_slice(sp, fc)
                    sPQ.wait(eng, (sp, fc))
                    if fc >= 4:
                        sRP.wait(eng, (sp, fc // 4 - 1))
                    i = eng.copy(pq_sb[:, (fc % 4) * 384:(fc % 4) * 384 + n],
                                 p_pq[fc % 2][:, 0:n])
                    sPQE.inc(i, (sp, fc))
        i = eng.nop()
        sAB.inc(i, ("s",))
        eng.wait_ge(sAB.h, 3)
        def _rstd_b(x, li):
            sBN.wait(eng, ("b", x, li))
            sl = li * 4 + (x % 2) * 2
            i = eng.activation(lnvB[:], bnagB[:, sl + 1:sl + 2],
                               AF.Ln, bias=eps_sb[:], scale=1.0)
            sa.edge(i, eng)
            i = eng.activation(rstdB[:, li * 2 + x % 2:li * 2 + x % 2 + 1],
                               lnvB[:], AF.Exp, bias=0.0, scale=-0.5)
            sRS.inc(i, ("b", x, li))

        for qb in range(KNQL + 2):
            po = qb - 1
            so = qb - 2
            esl = (qb % 2) * 6144

            def _exp(lo, hi):
                if qb >= KNQL:
                    return
                for ti in range(lo, hi):
                    kb, hg = TILES[ti]
                    sSC.wait(eng, (qb, kb, hg))
                    if ti == 0:
                        sPA.wait(eng, (qb - 2,))
                    i = eng.activation(
                        e_sb[:, esl + kb * 2048 + hg * 512:
                                esl + kb * 2048 + (hg + 1) * 512],
                        p_scA[:, (ti % 2) * 512:(ti % 2) * 512 + 512],
                        AF.Exp, bias=0.0, scale=0.125)
                    sE.inc(i, (qb, kb, hg))

            _exp(0, 10)
            if 0 <= po < KNQL:
                _rstd_b(po, 0)
            _exp(10, 12)
            if 0 <= po < KNQL:
                _rstd_b(po, 1)
            if qb < KNQL:
                sSM.wait(eng, (qb,))
                for pp in range(2):
                    pg = pp * 32
                    i = eng.activation(rtmp33[pg:pg + 1, :], p_scA[pg:pg + 1, 0:1024],
                                       AF.Ln, bias=0.0, scale=1.0)
                sRCln.inc(i, (qb,))
                sRCln.selfwait(eng)
                rrv = r_row[0:1, :].rearrange("o (a two b) -> o a two b",
                                              a=2, two=2, b=512)
                for pp in range(2):
                    pg = pp * 32
                    i = eng.activation(rrv[:, :, pp, :],
                                       rtmp33[pg:pg + 1, :].rearrange(
                                           "o (a b) -> o a b", a=2),
                                       AF.Exp, bias=0.0, scale=-1.0)
                sRC.inc(i, (qb,))
            if 0 <= so < KNQL:
                _rstd_b(so, 2)

    def em_vector(eng):
        eng.memset(ones_k[:], 1.0)
        eng.memset(ones_r[:], 1.0)
        i = eng.memset(eps_sb[:], EPS)
        sv.inc(i, ("init",))
        def _wshuf(m):
            sG[m].wait(eng, ("g", m))
            if m == 1:
                sWM.wait(eng, (0, 7))
            eng.tensor_copy(s_g1db[:].rearrange("c (d b) -> c d b", d=32),
                            s_g1[m][:].rearrange("c (b d) -> c d b", b=32))
            i = eng.tensor_copy(s_g2ge[:].rearrange("c (g e) -> c g e", g=16),
                                s_g2[m][:].rearrange("c (e g) -> c g e", e=16))
            svW.inc(i, ("sh", m))
        _wshuf(0)
        vix = 0
        def _stats(t):
            sXp[t % 2].wait(eng, (t,))
            sRS.wait(eng, (t - 2,))
            sl = t % 2
            sLN.selfwait(eng)
            eng.bn_stats(bn6[:, sl * 12:sl * 12 + 6],
                         xa[:, sl * 1024:sl * 1024 + 512])
            i = eng.bn_stats(bn6[:, sl * 12 + 6:sl * 12 + 12],
                             xa[:, sl * 1024 + 512:sl * 1024 + 1024])
            sv.edge(i, eng)
            i = eng.bn_aggr(bnag[:, sl * 2:sl * 2 + 2],
                            bn6[:, sl * 12:sl * 12 + 12])
            sBN.inc(i, (t,))
        def _apply(t):
            sl = t % 2
            sBN.selfwait(eng)
            sRS.wait(eng, (t,))
            sTP.wait(eng, (t - 1,))
            i = eng.tensor_scalar(xln[:], xa[:, sl * 1024:sl * 1024 + 1024],
                                  bnag[:, sl * 2:sl * 2 + 1],
                                  rstd[:, sl:sl + 1], ALU.subtract, ALU.mult)
            sLN.inc(i, (t,))
        for b in range(NSP + 1):
            if b == 3:
                _wshuf(1)
            tiles = (list(range(SPANS[b][0], SPANS[b][0] + SPANS[b][1]))
                     if b < NSP else [])
            ladder = {0: (lambda: _stats(tiles[0])),
                      1: (lambda: _stats(tiles[1])),
                      2: (lambda: _apply(tiles[0]))} if tiles else {}
            if 1 <= b:
                spv = b - 1
                t0p, ntp = SPANS[spv]
                k = 0
                for t in range(t0p, t0p + ntp):
                    for half in range(2):
                        if k in ladder:
                            ladder[k]()
                        k += 1
                        sQG.wait(eng, (spv, "v", t, half))
                        i = eng.tensor_copy(
                            v_tm[:, t * 1024 + half * 512:t * 1024 + (half + 1) * 512],
                            p_v[vix % 2][:])
                        sVE.inc(i, (vix,))
                        vix += 1
            else:
                for k in sorted(ladder):
                    ladder[k]()
            if tiles:
                _stats(tiles[2])
                _apply(tiles[1])
                _apply(tiles[2])
            sp = b - 1
            if 0 <= sp:
                sCS[sp % 2].wait(eng, ("cs", sp))
                t0p, ntp = SPANS[sp]
                qlo, qloc, qn = q_cols(sp)
                for g in range(4):
                    fc0 = g * 4
                    if fc0 < 8:
                        tl, n = qloc, qn
                        raw4 = q_fm[:].rearrange("p (c t) -> p c t", c=8)[
                            :, fc0:fc0 + 4, qlo * 128:qlo * 128 + n]
                    else:
                        tl, n = 0, ntp * 128
                        raw4 = k_fm[:].rearrange("p (c t) -> p c t", c=8)[
                            :, fc0 - 8:fc0 - 4, t0p * 128:t0p * 128 + n]
                    csb = crepS[:, tl:tl + n].unsqueeze(1).broadcast_to([128, 4, n])
                    ssb = srepS[:, tl:tl + n].unsqueeze(1).broadcast_to([128, 4, n])
                    pq4 = pq_sb[:].rearrange("p (s f) -> p s f", s=4)[:, :, 0:n]
                    m14 = m1b[:].rearrange("p (s f) -> p s f", s=4)[:, :, 0:n]
                    m24 = m2b[:].rearrange("p (s f) -> p s f", s=4)[:, :, 0:n]
                    sPQE.wait(eng, (sp, fc0 + 3))
                    sRP.selfwait(eng)
                    i = eng.tensor_tensor(m14, raw4, csb, ALU.mult)
                    sv.edge(i, eng)
                    i = eng.tensor_tensor(m24, pq4, ssb, ALU.mult)
                    sv.edge(i, eng)
                    i = eng.tensor_tensor(raw4, m14, m24, ALU.add)
                    sRP.inc(i, (sp, g))
        i = eng.nop()
        sAB.inc(i, ("v",))
        eng.wait_ge(sAB.h, 3)
        i = eng.memset(q_z[:], 0.0)
        sv.inc(i, ("qz",))
        sWL.wait(eng, (1,))
        i = eng.tensor_scalar_mul(wffn[:, 8 * 512:12 * 512],
                                  wffn[:, 4 * 512:8 * 512], -1.0)
        svW.inc(i, ("neg",))
        for qb in range(KNQL + 2):
            po = qb - 1
            so = qb - 2
            esl = (qb % 2) * 6144
            if 0 <= po < KNQL:
                sDR.wait(eng, (po,))
                i = eng.tensor_copy(rep_sb[:], p_scB[:])
                sv.edge(i, eng)
                sPA.wait(eng, (po,))
                sPJ.wait(eng, (po - 1,))
                eng.tensor_tensor(attn_fm[:, 0:512],
                                  p_att[:, 0:512], rep_sb[:, 0:512], ALU.mult)
                i = eng.tensor_tensor(attn_fm[:, 512:1024],
                                      p_att[:, 512:1024], rep_sb[:, 512:1024],
                                      ALU.mult)
                sAM.inc(i, (po,))
            if 0 <= po < KNQL:
                yo = (po % 2) * 1024
                sPJ.wait(eng, (po,))
                sl = 0 * 4 + (po % 2) * 2
                eng.bn_stats(bn6[:, 0:6], p_scB[:, 0:512])
                i = eng.bn_stats(bn6[:, 6:12], p_scB[:, 512:1024])
                sv.edge(i, eng)
                i = eng.bn_aggr(bnagB[:, sl:sl + 2], bn6[:, 0:12])
                sBN.inc(i, ("b", po, 0))
                sBN.selfwait(eng)
                sRS.wait(eng, ("b", po, 0))
                i = eng.tensor_scalar(negmr[:], bnagB[:, sl:sl + 1],
                                      rstdB[:, po % 2:po % 2 + 1], -1.0,
                                      ALU.mult, ALU.mult)
                sv.edge(i, eng)
                sXBp[po % 2].wait(eng, (po,))
                sOUTp[(po - 2) % 2].wait(eng, (po - 2,))
                i = eng.tensor_scalar(xs[:], xb[:, yo:yo + 1024], negmr[:],
                                      None, ALU.add)
                sv.edge(i, eng)
                eng.scalar_tensor_tensor(y1[:, yo:yo + 512], p_scB[:, 0:512],
                                         rstdB[:, po % 2:po % 2 + 1],
                                         xs[:, 0:512], ALU.mult, ALU.add)
                i = eng.scalar_tensor_tensor(y1[:, yo + 512:yo + 1024],
                                             p_scB[:, 512:1024],
                                             rstdB[:, po % 2:po % 2 + 1],
                                             xs[:, 512:1024], ALU.mult, ALU.add)
                sY1.inc(i, (po,))
                sY1.selfwait(eng)
            if 0 <= so < KNQL:
                sT2.wait(eng, (so,))
                sFF.wait(eng, (so - 1,))
                i = eng.tensor_copy(h2fm[:], p_tp2[:])
                sF2.inc(i, (so,))
            if 0 <= po < KNQL:
                yo = (po % 2) * 1024
                sl = 1 * 4 + (po % 2) * 2
                eng.bn_stats(bn6[:, 0:6], y1[:, yo:yo + 512])
                i = eng.bn_stats(bn6[:, 6:12], y1[:, yo + 512:yo + 1024])
                sv.edge(i, eng)
                i = eng.bn_aggr(bnagB[:, sl:sl + 2], bn6[:, 0:12])
                sBN.inc(i, ("b", po, 1))
                sBN.selfwait(eng)
                sRS.wait(eng, ("b", po, 1))
                sT2.wait(eng, (po - 1,))
                i = eng.tensor_scalar(h2[:], y1[:, yo:yo + 1024],
                                      bnagB[:, sl:sl + 1],
                                      rstdB[:, 2 + po % 2:2 + po % 2 + 1],
                                      ALU.subtract, ALU.mult)
                sH2.inc(i, (po,))
            if 0 <= so < KNQL:
                yo2 = (so % 2) * 1024
                sFF.wait(eng, (so,))
                sl = 2 * 4 + (so % 2) * 2
                eng.bn_stats(bn6[:, 12:18], p_smA[:])
                i = eng.bn_stats(bn6[:, 18:24], p_wb[:])
                sv.edge(i, eng)
                i = eng.bn_aggr(bnagB[:, sl:sl + 2], bn6[:, 12:24])
                sBN.inc(i, ("b", so, 2))
                sBN.selfwait(eng)
                sRS.wait(eng, ("b", so, 2))
                i = eng.tensor_scalar(negmr[:], bnagB[:, sl:sl + 1],
                                      rstdB[:, 4 + so % 2:4 + so % 2 + 1], -1.0,
                                      ALU.mult, ALU.mult)
                sv.edge(i, eng)
                sOUTp[(so - 1) % 2].wait(eng, (so - 1,))
                i = eng.tensor_scalar(xs[:], y1[:, yo2:yo2 + 1024], negmr[:],
                                      None, ALU.add)
                sv.edge(i, eng)
                eng.scalar_tensor_tensor(
                    xs[:, 0:512],
                    p_smA[:], rstdB[:, 4 + so % 2:4 + so % 2 + 1],
                    xs[:, 0:512], ALU.mult, ALU.add)
                i = eng.scalar_tensor_tensor(
                    xs[:, 512:1024],
                    p_wb[:], rstdB[:, 4 + so % 2:4 + so % 2 + 1],
                    xs[:, 512:1024], ALU.mult, ALU.add)
                sYO.inc(i, (so,))


    with ctx, nc.Block() as block:
        block.sync(em_sync)
        block.gpsimd(em_gpsimd)
        block.tensor(em_tensor)
        block.scalar(em_scalar)
        block.vector(em_vector)
    return nc


_CACHE = {}


def _get_nc():
    if "nc" not in _CACHE:
        ev = {}
        build_nc(ev)
        _CACHE["nc"] = build_nc(ev)
    return _CACHE["nc"]


def _host_constants():
    if "const" in _CACHE:
        return _CACHE["const"]
    idn = np.eye(128, dtype=np.float32).astype(BF)
    M = np.zeros((128, 128), np.float32)
    for blk in range(2):
        o = blk * 64
        for i in range(32):
            M[o + i, o + 32 + i] = -1.0
            M[o + 32 + i, o + i] = 1.0
    prot = np.ascontiguousarray(M.T).astype(BF)
    invf = 1.0 / (10000.0 ** (np.arange(0, 64, 2, dtype=np.float32) / 64.0))
    per_core = []
    for core in range(8):
        qt = core % 4
        lo = qt * TO - HW
        pos = np.clip(np.arange(lo, lo + TH), 0, 8191).astype(np.float32)
        fr = np.outer(pos, invf)
        crep = np.tile(np.ascontiguousarray(np.cos(fr).T), (4, 1)).astype(BF)
        srep = np.tile(np.ascontiguousarray(np.sin(fr).T), (4, 1)).astype(BF)
        j = np.arange(128)[:, None]
        r = np.arange(128)[None, :]
        msk = np.zeros((NQ, 2, 128, 128), np.float32)
        for qb in range(NQ):
            for mi, kb in ((0, 0), (1, 2)):
                band = (j >= r) if mi == 0 else (j <= r)
                gpos = lo + (qb + kb) * 128 + np.arange(128)
                ok = (gpos >= 0) & (gpos < 8192)
                msk[qb, mi] = band & ok[:, None]
        per_core.append((crep, srep, msk.astype(BF)))
    _CACHE["const"] = (idn, prot, per_core)
    return _CACHE["const"]


def kernel(L0_attn_x, L0_attn_Wqkv, L0_attn_Wout, L0_attn_bout,
           L0_ffn_G1_R, L0_ffn_G2_R, L0_ffn_G1_I, L0_ffn_G2_I,
           L0_ffn_bias_real, L0_ffn_bias_imag):
    x = np.ascontiguousarray(np.asarray(L0_attn_x, dtype=np.float32))
    idn, prot, per_core = _host_constants()
    shared = {
        "wqkv": np.ascontiguousarray(L0_attn_Wqkv, dtype=np.float32),
        "wout": np.ascontiguousarray(L0_attn_Wout, dtype=np.float32),
        "bout": np.ascontiguousarray(L0_attn_bout, dtype=np.float32),
        "g1r": np.ascontiguousarray(L0_ffn_G1_R, dtype=np.float32),
        "g2r": np.ascontiguousarray(L0_ffn_G2_R, dtype=np.float32),
        "g1i": np.ascontiguousarray(L0_ffn_G1_I, dtype=np.float32),
        "g2i": np.ascontiguousarray(L0_ffn_G2_I, dtype=np.float32),
        "br": np.ascontiguousarray(L0_ffn_bias_real, dtype=np.float32),
        "bi": np.ascontiguousarray(L0_ffn_bias_imag, dtype=np.float32),
        "idn": idn, "prot": prot,
    }
    in_maps = []
    for core in range(8):
        b, qt = core // 4, core % 4
        lo = qt * TO - HW
        xh = np.zeros((TH, D), np.float32)
        s0, s1 = max(lo, 0), min(lo + TH, 8192)
        xh[s0 - lo:s1 - lo] = x[b, s0:s1]
        crep, srep, msk = per_core[core]
        m = dict(shared)
        m.update({"xh": xh, "crep": crep, "srep": srep, "msk": msk})
        in_maps.append(m)
    nc = _get_nc()
    res = run_bass_kernel_spmd(nc, in_maps, core_ids=list(range(8)))
    out = np.empty((2, 8192, D), np.float32)
    for core in range(8):
        b, qt = core // 4, core % 4
        out[b, qt * TO:(qt + 1) * TO] = res.results[core]["y"]
    return out


# version 31
# speedup vs baseline: 1.0113x; 1.0113x over previous
import os
import numpy as np
import ml_dtypes
from contextlib import ExitStack

import concourse.bass as bass
import concourse.mybir as mybir
from concourse.bass_utils import run_bass_kernel_spmd

F32 = mybir.dt.float32
BF16 = mybir.dt.bfloat16
AF = mybir.ActivationFunctionType
ALU = mybir.AluOpType
BF = ml_dtypes.bfloat16

D = 1024
NH, HD = 16, 64
TO = 2048
HW = 128
TH = TO + 2 * HW
NT = TH // 128
NQ = TO // 128
EPS = 1e-5
KNQL = int(os.environ.get("KNQL", "16"))
KPHA = int(os.environ.get("KPHA", "1"))
KSEC = int(os.environ.get("KSEC", "8"))
SPANS = [(3 * i, 3) for i in range(6)]
NSP = len(SPANS)
TILES = [(kb, hg) for kb in (0, 2, 1) for hg in range(4)]


class Sem:
    def __init__(self, ctx, nc, name, ev):
        self.h = ctx.enter_context(nc.semaphore(name))
        self.n = 0
        self.name = name
        self.ev = ev

    def inc(self, inst, key=None, k=1):
        self.n += k
        if inst is not None and k > 0:
            inst.then_inc(self.h, k)
        if key is not None:
            self.ev[(self.name,) + tuple(key)] = self.n
        return self.n

    def wait(self, eng, key):
        t = self.ev.get((self.name,) + tuple(key), 0)
        if t > 0:
            eng.wait_ge(self.h, t)

    def selfwait(self, eng):
        eng.wait_ge(self.h, self.n)

    def edge(self, inst, eng):
        self.n += 1
        inst.then_inc(self.h, 1)
        eng.wait_ge(self.h, self.n)


def q_range(t0, nt):
    return max(t0 - 1, 0), min(t0 + nt - 1, NQ)


def q_cols(si):
    t0, nt = SPANS[si]
    qlo, qhi = q_range(t0, nt)
    return qlo, (qlo + 1 - t0) * 128, (qhi - qlo) * 128


def build_nc(ev):
    nc = bass.Bass("TRN2", target_bir_lowering=False)

    d_xh = nc.dram_tensor("xh", [TH, D], F32, kind="ExternalInput").ap()
    d_wqkv = nc.dram_tensor("wqkv", [D, 3 * D], F32, kind="ExternalInput").ap()
    d_wout = nc.dram_tensor("wout", [D, D], F32, kind="ExternalInput").ap()
    d_bout = nc.dram_tensor("bout", [D], F32, kind="ExternalInput").ap()
    d_g1 = [nc.dram_tensor("g1r", [1, 32, 4, 32], F32, kind="ExternalInput").ap(),
            nc.dram_tensor("g1i", [1, 32, 4, 32], F32, kind="ExternalInput").ap()]
    d_g2 = [nc.dram_tensor("g2r", [4, 16, 1, 16], F32, kind="ExternalInput").ap(),
            nc.dram_tensor("g2i", [4, 16, 1, 16], F32, kind="ExternalInput").ap()]
    d_br = nc.dram_tensor("br", [512], F32, kind="ExternalInput").ap()
    d_bi = nc.dram_tensor("bi", [512], F32, kind="ExternalInput").ap()
    d_idn = nc.dram_tensor("idn", [128, 128], BF16, kind="ExternalInput").ap()
    d_prot = nc.dram_tensor("prot", [128, 128], BF16, kind="ExternalInput").ap()
    d_crep = nc.dram_tensor("crep", [128, TH], BF16, kind="ExternalInput").ap()
    d_srep = nc.dram_tensor("srep", [128, TH], BF16, kind="ExternalInput").ap()
    d_msk = nc.dram_tensor("msk", [NQ, 2, 128, 128], BF16, kind="ExternalInput").ap()
    d_y = nc.dram_tensor("y", [TO, D], F32, kind="ExternalOutput").ap()
    d_scr = nc.dram_tensor("wscr", [2, 1024, 256], BF16, kind="Internal").ap()

    ctx = ExitStack()
    sbr = lambda n, s, d: ctx.enter_context(nc.sbuf_tensor(n, s, d))

    q_fm = sbr("q_fm", [128, 8 * TO], BF16)
    k_fm = sbr("k_fm", [128, 8 * TH], BF16)
    v_tm = sbr("v_tm", [128, NT * D], BF16)
    wffn = sbr("wffn", [128, 12 * 512], BF16)
    idn_sb = sbr("idn_sb", [128, 128], BF16)
    prot_sb = sbr("prot_sb", [128, 128], BF16)
    ones_k = sbr("ones_k", [128, 1], BF16)
    ones_r = sbr("ones_r", [1, 128], BF16)
    msk_sb = sbr("msk_sb", [128, 2 * 256], BF16)
    bn6 = sbr("bn6", [128, 2 * 12], F32)
    bnag = sbr("bnag", [128, 2 * 2], F32)
    lnv = sbr("lnv", [128, 1], F32)
    rstd = sbr("rstd", [128, 2], F32)
    eps_sb = sbr("eps_sb", [128, 1], F32)
    p_wb = ctx.enter_context(nc.psum_tensor("p_wb", [128, 512], F32))

    ctxA = ExitStack()
    sba = lambda n, s, d: ctxA.enter_context(nc.sbuf_tensor(n, s, d))
    psa = lambda n, s, d: ctxA.enter_context(nc.psum_tensor(n, s, d))
    wqkv_sb = sba("wqkv_sb", [128, 8 * 3072], BF16)
    s_g1 = [sba("s_g1r", [4, 1024], BF16), sba("s_g1i", [4, 1024], BF16)]
    s_g2 = [sba("s_g2r", [4, 256], BF16), sba("s_g2i", [4, 256], BF16)]
    s_g1db = sba("s_g1db", [4, 1024], BF16)
    s_g2ge = sba("s_g2ge", [4, 256], BF16)
    s_pw = sba("s_pw", [128, 2 * 256], BF16)
    crepS = sba("crepS", [128, 384], BF16)
    srepS = sba("srepS", [128, 384], BF16)
    xfm = sba("xfm", [128, 2 * 8 * 384], BF16)
    xa = sba("xa", [128, 2 * 1024], F32)
    xln = sba("xln", [128, 1024], BF16)
    m1b = sba("m1b", [128, 4 * 384], BF16)
    m2b = sba("m2b", [128, 4 * 384], BF16)
    pq_sb = sba("pq_sb", [128, 4 * 384], BF16)
    p_tp = [psa("p_tp0", [128, 1024], BF16)]
    p_qk = [psa("p_qk0", [128, 512], F32), psa("p_qk1", [128, 512], F32)]
    p_v = [psa("p_v0", [128, 512], F32), psa("p_v1", [128, 512], F32)]
    p_pq = [psa("p_pq0", [128, 512], F32), psa("p_pq1", [128, 512], F32)]
    ctxA.close()

    ctxB = ExitStack()
    sbb = lambda n, s, d: ctxB.enter_context(nc.sbuf_tensor(n, s, d))
    psb = lambda n, s, d: ctxB.enter_context(nc.psum_tensor(n, s, d))
    wout_sb = sbb("wout_sb", [128, 8 * 1024], BF16)
    bias_sb = sbb("bias_sb", [1, 2048], BF16)
    e_sb = sbb("e_sb", [128, 2 * 3 * 2048], BF16)
    xb = sbb("xb", [128, 2 * 1024], F32)
    y1 = sbb("y1", [128, 2 * 1024], F32)
    h2 = sbb("h2", [128, 1024], BF16)
    h2fm = sbb("h2fm", [128, 8 * 128], BF16)
    attn_fm = sbb("attn_fm", [128, 1024], BF16)
    rep_sb = sbb("rep_sb", [128, 1024], BF16)
    r_row = sbb("r_row", [1, 2048], BF16)
    rtmp33 = sbb("rtmp33", [33, 1024], F32)
    q_z = sbb("q_z", [128, 2 * 2048], BF16)
    xs = sbb("xs", [128, 1024], F32)
    bnagB = sbb("bnagB", [128, 3 * 2 * 2], F32)
    rstdB = sbb("rstdB", [128, 3 * 2], F32)
    lnvB = sbb("lnvB", [128, 1], F32)
    negmr = sbb("negmr", [128, 1], F32)
    p_scA = psb("p_scA", [128, 1024], F32)
    p_scB = psb("p_scB", [128, 1024], F32)
    p_att = psb("p_att", [128, 1024], F32)
    p_smA = psb("p_smA", [128, 512], F32)
    ctxB.close()

    p_tp2 = p_smA[:].bitcast(BF16)

    S = lambda n: Sem(ctx, nc, n, ev)
    sW = S("sW"); sCC = S("sCC")
    sCS = [S("sCS0"), S("sCS1")]
    sXp = [S("sX0"), S("sX1")]
    sBN = S("sBN"); sRS = S("sRS")
    sLN = S("sLN"); sTP = S("sTP"); sFM = S("sFM"); sQG = S("sQG")
    sQE = S("sQE"); sPQ = S("sPQ"); sPQE = S("sPQE"); sRP = S("sRP"); sVE = S("sVE")
    sAB = S("sAB"); sWM = S("sWM"); sWE = S("sWE")
    sWSp = [S("sWS0"), S("sWS1")]; sWL = S("sWL")
    sG = [S("sG0"), S("sG1")]; sBIA = S("sBIA"); sWO = S("sWO")
    sXBp = [S("sXB0"), S("sXB1")]; sMKp = [S("sMK0"), S("sMK1")]
    sQZp = [S("sQZ0"), S("sQZ1")]
    sSC = S("sSC"); sE = S("sE"); sEM = S("sEM")
    sSM = S("sSM"); sRC = S("sRC"); sRCln = S("sRCln"); sDR = S("sDR")
    sPA = S("sPA"); sAM = S("sAM"); sPJ = S("sPJ"); sY1 = S("sY1")
    sH2 = S("sH2"); sT2 = S("sT2"); sF2 = S("sF2"); sFF = S("sFF")
    sYO = S("sYO"); sOUTp = [S("sOUT0"), S("sOUT1")]
    sv = S("sv"); sa = S("sa"); svW = S("svW")

    def rope_slice(sp, fc):
        t0, nt = SPANS[sp]
        qlo, qloc, qn = q_cols(sp)
        if fc < 8:
            return (q_fm[:, fc * TO + qlo * 128:fc * TO + qlo * 128 + qn], qloc, qn)
        kc = fc - 8
        return (k_fm[:, kc * TH + t0 * 128:kc * TH + (t0 + nt) * 128], 0, nt * 128)

    def em_sync(eng):
        i = eng.dma_start(out=idn_sb[:], in_=d_idn)
        sCC.inc(i, None, 16)
        i = eng.dma_start(out=prot_sb[:], in_=d_prot)
        sCC.inc(i, ("cc",), 16)
        for si, (t0, nt) in enumerate(SPANS):
            for t in range(t0, t0 + nt):
                sLN.wait(eng, (t - 2,))
                i = eng.dma_start(out=xa[:, (t % 2) * 1024:(t % 2) * 1024 + 1024],
                                  in_=d_xh[t * 128:(t + 1) * 128, :])
                sXp[t % 2].inc(i, (t,), 16)
            sRP.wait(eng, (si - 1, 3))
            i = eng.dma_start(out=crepS[:],
                              in_=d_crep[:, t0 * 128:t0 * 128 + 384])
            sCS[si % 2].inc(i, None, 16)
            i = eng.dma_start(out=srepS[:],
                              in_=d_srep[:, t0 * 128:t0 * 128 + 384])
            sCS[si % 2].inc(i, ("cs", si), 16)
        eng.wait_ge(sAB.h, 3)
        for qb in range(KNQL + 2):
            if qb < KNQL:
                sSC.wait(eng, (qb - 2, 1, 3))
                if qb < 2:
                    sv.wait(eng, ("qz",))
                qcols = q_fm[:].rearrange("p (c t) -> p c t", c=8)[:, :, qb * 128:(qb + 1) * 128]
                zslot = (qb % 2) * 2048
                qzv = q_z[:, zslot:zslot + 2048].rearrange(
                    "p (c two t) -> p c two t", c=8, two=2)
                i = eng.dma_start(out=qzv[0:64, :, 0, :], in_=qcols[0:64])
                sQZp[qb % 2].inc(i, None, 16)
                i = eng.dma_start(out=qzv[64:128, :, 1, :], in_=qcols[64:128])
                sQZp[qb % 2].inc(i, (qb,), 16)
                sEM.wait(eng, (qb - 2, 2))
                i = eng.dma_start(
                    out=msk_sb[:, (qb % 2) * 256:(qb % 2) * 256 + 256]
                        .rearrange("p (mi f) -> p mi f", mi=2),
                    in_=d_msk[qb].rearrange("mi p f -> p mi f"))
                sMKp[qb % 2].inc(i, (qb,), 16)
                sY1.wait(eng, (qb - 2,))
                i = eng.dma_start(out=xb[:, (qb % 2) * 1024:(qb % 2) * 1024 + 1024],
                                  in_=d_xh[HW + qb * 128:HW + (qb + 1) * 128, :])
                sXBp[qb % 2].inc(i, (qb,), 16)

    def em_gpsimd(eng):
        for m in range(2):
            i = eng.dma_start(out=s_g1[m][:].rearrange("c (b d) -> c b d", b=32),
                              in_=d_g1[m][0].rearrange("b c d -> c b d"))
            sG[m].inc(i, None, 16)
            i = eng.dma_start(out=s_g2[m][:],
                              in_=d_g2[m][:, :, 0, :].rearrange("c e g -> c (e g)"))
            sG[m].inc(i, ("g", m), 16)
        for t3 in range(3):
            for dc in range(8):
                i = eng.dma_start(
                    out=wqkv_sb[:, dc * 3072 + t3 * 1024:
                                   dc * 3072 + (t3 + 1) * 1024],
                    in_=d_wqkv[dc * 128:(dc + 1) * 128,
                               t3 * 1024:(t3 + 1) * 1024])
                sW.inc(i, ("t3", t3) if dc == 7 else None, 16)
        for m in range(2):
            for mt in range(8):
                r = 8 * m + mt
                sWE.wait(eng, (m, mt))
                i = eng.dma_start(out=d_scr[m, mt * 128:(mt + 1) * 128, :],
                                  in_=s_pw[:, (r % 2) * 256:(r % 2) * 256 + 256])
                sWSp[r % 2].inc(i, (r,), 16)
            sWSp[0].wait(eng, (8 * m + 6,))
            sWSp[1].wait(eng, (8 * m + 7,))
            scr_r = d_scr[m].rearrange("(d b) (g e) -> d g b e", d=32, b=32, g=16)
            base = 0 if m == 0 else 4
            for ci in range(4):
                for dr in range(8):
                    i = eng.dma_start(
                        out=wffn[dr * 16:(dr + 1) * 16,
                                 (base + ci) * 512:(base + ci + 1) * 512],
                        in_=scr_r[ci * 8 + dr])
                    sWL.inc(i, None, 16)
            sWL.inc(None, (m,), 0)
        eng.wait_ge(sAB.h, 3)
        i = eng.dma_start(out=bias_sb[0:1, 0:1024], in_=d_bout.unsqueeze(0))
        sBIA.inc(i, None, 16)
        i = eng.dma_start(out=bias_sb[0:1, 1024:1536], in_=d_br.unsqueeze(0))
        sBIA.inc(i, None, 16)
        i = eng.dma_start(out=bias_sb[0:1, 1536:2048], in_=d_bi.unsqueeze(0))
        sBIA.inc(i, ("bias",), 16)
        for dc in range(8):
            i = eng.dma_start(out=wout_sb[:, dc * 1024:(dc + 1) * 1024],
                              in_=d_wout[dc * 128:(dc + 1) * 128, :])
            sWO.inc(i, ("wout",) if dc == 7 else None, 16)
        for qb in range(KNQL + 2):
            so = qb - 2
            if qb < KNQL:
                esl = (qb % 2) * 6144
                for hg in range(4):
                    sE.wait(eng, (qb, 0, hg))
                sMKp[qb % 2].wait(eng, (qb,))
                mk0 = msk_sb[:, (qb % 2) * 256:(qb % 2) * 256 + 128]
                ev0 = e_sb[:, esl:esl + 2048].rearrange("p (h f) -> p h f", h=16)
                i = eng.tensor_tensor(
                    ev0, ev0, mk0.unsqueeze(1).broadcast_to([128, 16, 128]),
                    ALU.mult)
                sEM.inc(i, (qb, 0))
                for hg in range(4):
                    sE.wait(eng, (qb, 2, hg))
                mk2 = msk_sb[:, (qb % 2) * 256 + 128:(qb % 2) * 256 + 256]
                ev2 = e_sb[:, esl + 4096:esl + 6144].rearrange("p (h f) -> p h f", h=16)
                i = eng.tensor_tensor(
                    ev2, ev2, mk2.unsqueeze(1).broadcast_to([128, 16, 128]),
                    ALU.mult)
                sEM.inc(i, (qb, 2))
            if 0 <= so < KNQL:
                sYO.wait(eng, (so,))
                i = eng.dma_start(out=d_y[so * 128:(so + 1) * 128, :], in_=xs[:])
                sOUTp[so % 2].inc(i, (so,), 16)
        for sp_ in sOUTp:
            if sp_.n > 0:
                eng.wait_ge(sp_.h, sp_.n)

    def em_tensor(eng):
        wrounds = [(m, mt) for m in range(2) for mt in range(8)]
        WSL = [(0, 3), (3, 6), (6, 8), (8, 11), (11, 14), (14, 16), (16, 16)]

        def _wmm(b):
            for m, mt in wrounds[WSL[b][0]:WSL[b][1]]:
                if mt == 0:
                    svW.wait(eng, ("sh", m))
                if mt >= 1:
                    sWE.wait(eng, (m, mt - 1))
                elif m == 1:
                    sWE.wait(eng, (0, 7))
                i = eng.matmul(p_wb[:, 0:256], s_g1db[:, mt * 128:(mt + 1) * 128],
                               s_g2ge[:], start=True, stop=True)
                sWM.inc(i, (m, mt))
        sCC.wait(eng, ("cc",))
        vix = 0
        for b in range(NSP + 1):
            _wmm(b)
            if b < NSP:
                si = b
                t0, nt = SPANS[si]
                for t in range(t0, t0 + nt):
                    sLN.wait(eng, (t,))
                    sFM.wait(eng, (t - 1,))
                    for c in range(8):
                        i = eng.transpose(p_tp[0][:, c * 128:(c + 1) * 128],
                                          xln[:, c * 128:(c + 1) * 128], idn_sb[:])
                    sTP.inc(i, (t,))
                sFM.wait(eng, (t0 + nt - 1,))
                qlo, qloc, qn = q_cols(si)
                for fc in range(16):
                    if si == 0 and fc in (0, 8):
                        sW.wait(eng, ("t3", fc // 8))
                    bank = p_qk[fc % 2]
                    if fc >= 2:
                        sQE.wait(eng, (si, fc - 2))
                    else:
                        sQE.wait(eng, (si - 1, 14 + fc))
                    off, n = (qloc, qn) if fc < 8 else (0, nt * 128)
                    for dc in range(8):
                        i = eng.matmul(
                            bank[:, 0:n],
                            wqkv_sb[:, dc * 3072 + fc * 128:dc * 3072 + (fc + 1) * 128],
                            xfm[:, (si % 2) * 3072 + dc * 384 + off:
                                   (si % 2) * 3072 + dc * 384 + off + n],
                            start=(dc == 0), stop=(dc == 7))
                    sQG.inc(i, (si, fc))
                    if si >= 1:
                        sp = si - 1
                        raw, _, pn = rope_slice(sp, fc)
                        sQE.wait(eng, (sp, fc))
                        sPQE.wait(eng, (sp, fc - 2))
                        if fc < 2:
                            sPQE.wait(eng, (sp - 1, 14 + fc))
                        i = eng.matmul(p_pq[fc % 2][:, 0:pn], prot_sb[:], raw,
                                       start=True, stop=True)
                        sPQ.inc(i, (sp, fc))
                for t in range(t0, t0 + nt):
                    for half in range(2):
                        if si == 0 and t == t0 and half == 0:
                            sW.wait(eng, ("t3", 2))
                        bank = p_v[vix % 2]
                        sVE.wait(eng, (vix - 2,))
                        for dc in range(8):
                            i = eng.matmul(
                                bank[:],
                                xfm[:, (si % 2) * 3072 + dc * 384 + (t - t0) * 128:
                                       (si % 2) * 3072 + dc * 384 + (t - t0) * 128 + 128],
                                wqkv_sb[:, dc * 3072 + 2048 + half * 512:
                                           dc * 3072 + 2048 + (half + 1) * 512],
                                start=(dc == 0), stop=(dc == 7))
                        sQG.inc(i, (si, "v", t, half))
                        vix += 1
                sQG.inc(None, (si, "vend"), 0)
            if b == NSP:
                sp = NSP - 1
                for fc in range(16):
                    raw, _, pn = rope_slice(sp, fc)
                    sQE.wait(eng, (sp, fc))
                    sPQE.wait(eng, (sp, fc - 2))
                    if fc < 2:
                        sPQE.wait(eng, (sp - 1, 14 + fc))
                    i = eng.matmul(p_pq[fc % 2][:, 0:pn], prot_sb[:], raw,
                                   start=True, stop=True)
                    sPQ.inc(i, (sp, fc))
        i = eng.nop()
        sAB.inc(i, ("t",))
        eng.wait_ge(sAB.h, 3)
        for qb in range(KNQL + 2):
            po = qb - 1
            so = qb - 2

            def _sc(lo, hi):
                if qb >= KNQL:
                    return
                if qb < 2 and lo == 0:
                    sv.wait(eng, ("qz",))
                for ti in range(lo, hi):
                    kb, hg = TILES[ti]
                    bank = p_scA[:, (ti % 2) * 512:(ti % 2) * 512 + 512]
                    if ti >= 2:
                        pkb, phg = TILES[ti - 2]
                        sE.wait(eng, (qb, pkb, phg))
                    else:
                        pkb, phg = TILES[10 + ti]
                        sE.wait(eng, (qb - 1, pkb, phg))
                        sRCln.wait(eng, (qb - 1,))
                    if ti == 0:
                        sQZp[qb % 2].wait(eng, (qb,))
                    for cc in range(2):
                        c = hg * 2 + cc
                        i = eng.matmul(
                            bank[:, cc * 256:(cc + 1) * 256],
                            k_fm[:, c * TH + (qb + kb) * 128:
                                    c * TH + (qb + kb) * 128 + 128],
                            q_z[:, (qb % 2) * 2048 + c * 256:
                                   (qb % 2) * 2048 + (c + 1) * 256],
                            start=True, stop=True)
                    sSC.inc(i, (qb, kb, hg))

            if 0 <= po < KNQL:
                sEM.wait(eng, (po, 0))
                sEM.wait(eng, (po, 2))
                for hg in range(4):
                    sE.wait(eng, (po, 1, hg))
                sAM.wait(eng, (po - 1,))
                esl = (po % 2) * 6144
                for c in range(8):
                    for par in range(2):
                        h = 2 * c + par
                        for kb in range(3):
                            i = eng.matmul(
                                p_att[par * 64:(par + 1) * 64, c * 128:(c + 1) * 128],
                                v_tm[:, (po + kb) * 1024 + h * 64:
                                        (po + kb) * 1024 + (h + 1) * 64],
                                e_sb[:, esl + kb * 2048 + h * 128:
                                        esl + kb * 2048 + (h + 1) * 128],
                                start=(kb == 0), stop=(kb == 2),
                                tile_position=(0, 64) if par else None)
                sPA.inc(i, (po,))
            _sc(0, 2)
            if 0 <= po < KNQL:
                sRC.wait(eng, (po,))
                sY1.wait(eng, (po - 1,))
                for c in range(8):
                    for par in range(2):
                        h = 2 * c + par
                        i = eng.matmul(
                            p_scB[par * 64:(par + 1) * 64, c * 128:(c + 1) * 128],
                            ones_r[0:1, 0:64],
                            r_row[0:1, h * 128:(h + 1) * 128],
                            start=True, stop=True,
                            tile_position=(0, 64) if par else None)
                sDR.inc(i, (po,))
            _sc(2, 6)
            if 0 <= po < KNQL:
                sAM.wait(eng, (po,))
                sY1.wait(eng, (po - 1,))
                if po == 0:
                    sWO.wait(eng, ("wout",))
                    sBIA.wait(eng, ("bias",))
                    sWE.wait(eng, (1, 7))
                for half in range(2):
                    for c in range(8):
                        eng.matmul(
                            p_scB[:, half * 512:(half + 1) * 512],
                            attn_fm[:, c * 128:(c + 1) * 128],
                            wout_sb[:, c * 1024 + half * 512:c * 1024 + (half + 1) * 512],
                            start=(c == 0), stop=False)
                    i = eng.matmul(p_scB[:, half * 512:(half + 1) * 512],
                                   ones_r[0:1, :],
                                   bias_sb[0:1, half * 512:(half + 1) * 512],
                                   start=False, stop=True)
                sPJ.inc(i, (po,))
            _sc(6, 10)
            if 0 <= so < KNQL:
                sH2.wait(eng, (so,))
                sYO.wait(eng, (so - 1,))
                for c in range(8):
                    i = eng.transpose(p_tp2[:, c * 128:(c + 1) * 128],
                                      h2[:, c * 128:(c + 1) * 128],
                                      idn_sb[:])
                sT2.inc(i, (so,))
            _sc(10, 12)
            if 0 <= so < KNQL:
                sF2.wait(eng, (so,))
                if so == 0:
                    svW.wait(eng, ("neg",))
                for half in range(2):
                    tgt = p_smA[:] if half == 0 else p_wb[:]
                    slots = ([0, 8] if half == 0 else [4, 0])
                    for ci in range(4):
                        eng.matmul(
                            tgt,
                            h2fm[:, ci * 128:(ci + 1) * 128],
                            wffn[:, (slots[0] + ci) * 512:(slots[0] + ci + 1) * 512],
                            start=(ci == 0), stop=False)
                    for ci in range(4):
                        eng.matmul(
                            tgt,
                            h2fm[:, (4 + ci) * 128:(4 + ci + 1) * 128],
                            wffn[:, (slots[1] + ci) * 512:(slots[1] + ci + 1) * 512],
                            start=False, stop=False)
                    i = eng.matmul(tgt,
                                   ones_r[0:1, :],
                                   bias_sb[0:1, 1024 + half * 512:1024 + (half + 1) * 512],
                                   start=False, stop=True)
                sFF.inc(i, (so,))
            if qb < KNQL:
                sE.wait(eng, (qb, TILES[10][0], TILES[10][1]))
                sE.wait(eng, (qb, TILES[11][0], TILES[11][1]))
                sEM.wait(eng, (qb, 0))
                sEM.wait(eng, (qb, 2))
                for hg in range(4):
                    sE.wait(eng, (qb, 1, hg))
                esl2 = (qb % 2) * 6144
                for g in range(4):
                    pg = (g % 2) * 32
                    for kb in range(3):
                        i = eng.matmul(
                            p_scA[pg:pg + 1, (g // 2) * 512:(g // 2) * 512 + 512],
                            ones_k[:],
                            e_sb[:, esl2 + kb * 2048 + g * 512:
                                    esl2 + kb * 2048 + (g + 1) * 512],
                            start=(kb == 0), stop=(kb == 2))
                sSM.inc(i, (qb,))

    def em_scalar(eng):
        sv.wait(eng, ("init",))
        wrounds = [(m, mt) for m in range(2) for mt in range(8)]
        WSL = [(0, 3), (3, 6), (6, 8), (8, 11), (11, 14), (14, 16), (16, 16)]

        def _wevac(b):
            for m, mt in wrounds[WSL[b][0]:WSL[b][1]]:
                r = 8 * m + mt
                sWM.wait(eng, (m, mt))
                if r >= 2:
                    sWSp[r % 2].wait(eng, (r - 2,))
                i = eng.copy(s_pw[:, (r % 2) * 256:(r % 2) * 256 + 256],
                             p_wb[:, 0:256])
                sWE.inc(i, (m, mt))
        def _rstd(b, t):
            sLN.wait(eng, (t - 2,))
            sBN.wait(eng, (t,))
            sl = t % 2
            i = eng.activation(lnv[:], bnag[:, 2 * sl + 1:2 * sl + 2],
                               AF.Ln, bias=eps_sb[:], scale=1.0)
            sa.edge(i, eng)
            i = eng.activation(rstd[:, sl:sl + 1], lnv[:], AF.Exp,
                               bias=0.0, scale=-0.5)
            sRS.inc(i, (t,))
        for b in range(NSP + 1):
            if b < NSP:
                t0, nt = SPANS[b]
                _rstd(b, t0)
                _rstd(b, t0 + 1)
            _wevac(b)
            if b < NSP:
                _rstd(b, t0 + 2)
                for t in range(t0, t0 + nt):
                    sTP.wait(eng, (t,))
                    sQG.wait(eng, (b - 2, "vend"))
                    i = eng.copy(
                        xfm[:, (b % 2) * 3072:(b % 2) * 3072 + 3072]
                            .rearrange("p (c f) -> p c f", c=8)
                            [:, :, (t - t0) * 128:(t - t0) * 128 + 128],
                        p_tp[0][:].rearrange("p (c f) -> p c f", c=8))
                    sFM.inc(i, (t,))
            for fc in range(16):
                if b < NSP:
                    raw, _, n = rope_slice(b, fc)
                    sQG.wait(eng, (b, fc))
                    i = eng.copy(raw, p_qk[fc % 2][:, 0:n])
                    sQE.inc(i, (b, fc))
                if b >= 1:
                    sp = b - 1
                    _, _, n = rope_slice(sp, fc)
                    sPQ.wait(eng, (sp, fc))
                    if fc >= 4:
                        sRP.wait(eng, (sp, fc // 4 - 1))
                    i = eng.copy(pq_sb[:, (fc % 4) * 384:(fc % 4) * 384 + n],
                                 p_pq[fc % 2][:, 0:n])
                    sPQE.inc(i, (sp, fc))
        i = eng.nop()
        sAB.inc(i, ("s",))
        eng.wait_ge(sAB.h, 3)
        def _rstd_b(x, li):
            sBN.wait(eng, ("b", x, li))
            sl = li * 4 + (x % 2) * 2
            i = eng.activation(lnvB[:], bnagB[:, sl + 1:sl + 2],
                               AF.Ln, bias=eps_sb[:], scale=1.0)
            sa.edge(i, eng)
            i = eng.activation(rstdB[:, li * 2 + x % 2:li * 2 + x % 2 + 1],
                               lnvB[:], AF.Exp, bias=0.0, scale=-0.5)
            sRS.inc(i, ("b", x, li))

        for qb in range(KNQL + 2):
            po = qb - 1
            so = qb - 2
            esl = (qb % 2) * 6144

            def _exp(lo, hi):
                if qb >= KNQL:
                    return
                for ti in range(lo, hi):
                    kb, hg = TILES[ti]
                    sSC.wait(eng, (qb, kb, hg))
                    if ti == 0:
                        sPA.wait(eng, (qb - 2,))
                    i = eng.activation(
                        e_sb[:, esl + kb * 2048 + hg * 512:
                                esl + kb * 2048 + (hg + 1) * 512],
                        p_scA[:, (ti % 2) * 512:(ti % 2) * 512 + 512],
                        AF.Exp, bias=0.0, scale=0.125)
                    sE.inc(i, (qb, kb, hg))

            _exp(0, 10)
            if 0 <= po < KNQL:
                _rstd_b(po, 0)
            _exp(10, 12)
            if 0 <= po < KNQL:
                _rstd_b(po, 1)
            if qb < KNQL:
                sSM.wait(eng, (qb,))
                i = eng.activation(rtmp33[:], p_scA[0:33, 0:1024],
                                   AF.Ln, bias=0.0, scale=1.0)
                sRCln.inc(i, (qb,))
                sRCln.selfwait(eng)
                rrv = r_row[0:1, :].rearrange("o (a two b) -> o a two b",
                                              a=2, two=2, b=512)
                for pp in range(2):
                    pg = pp * 32
                    i = eng.activation(rrv[:, :, pp, :],
                                       rtmp33[pg:pg + 1, :].rearrange(
                                           "o (a b) -> o a b", a=2),
                                       AF.Exp, bias=0.0, scale=-1.0)
                sRC.inc(i, (qb,))
            if 0 <= so < KNQL:
                _rstd_b(so, 2)

    def em_vector(eng):
        eng.memset(ones_k[:], 1.0)
        eng.memset(ones_r[:], 1.0)
        i = eng.memset(eps_sb[:], EPS)
        sv.inc(i, ("init",))
        def _wshuf(m):
            sG[m].wait(eng, ("g", m))
            if m == 1:
                sWM.wait(eng, (0, 7))
            eng.tensor_copy(s_g1db[:].rearrange("c (d b) -> c d b", d=32),
                            s_g1[m][:].rearrange("c (b d) -> c d b", b=32))
            i = eng.tensor_copy(s_g2ge[:].rearrange("c (g e) -> c g e", g=16),
                                s_g2[m][:].rearrange("c (e g) -> c g e", e=16))
            svW.inc(i, ("sh", m))
        _wshuf(0)
        vix = 0
        def _stats(t):
            sXp[t % 2].wait(eng, (t,))
            sRS.wait(eng, (t - 2,))
            sl = t % 2
            sLN.selfwait(eng)
            eng.bn_stats(bn6[:, sl * 12:sl * 12 + 6],
                         xa[:, sl * 1024:sl * 1024 + 512])
            i = eng.bn_stats(bn6[:, sl * 12 + 6:sl * 12 + 12],
                             xa[:, sl * 1024 + 512:sl * 1024 + 1024])
            sv.edge(i, eng)
            i = eng.bn_aggr(bnag[:, sl * 2:sl * 2 + 2],
                            bn6[:, sl * 12:sl * 12 + 12])
            sBN.inc(i, (t,))
        def _apply(t):
            sl = t % 2
            sBN.selfwait(eng)
            sRS.wait(eng, (t,))
            sTP.wait(eng, (t - 1,))
            i = eng.tensor_scalar(xln[:], xa[:, sl * 1024:sl * 1024 + 1024],
                                  bnag[:, sl * 2:sl * 2 + 1],
                                  rstd[:, sl:sl + 1], ALU.subtract, ALU.mult)
            sLN.inc(i, (t,))
        for b in range(NSP + 1):
            if b == 3:
                _wshuf(1)
            tiles = (list(range(SPANS[b][0], SPANS[b][0] + SPANS[b][1]))
                     if b < NSP else [])
            ladder = {0: (lambda: _stats(tiles[0])),
                      1: (lambda: _stats(tiles[1])),
                      2: (lambda: _apply(tiles[0]))} if tiles else {}
            if 1 <= b:
                spv = b - 1
                t0p, ntp = SPANS[spv]
                k = 0
                for t in range(t0p, t0p + ntp):
                    for half in range(2):
                        if k in ladder:
                            ladder[k]()
                        k += 1
                        sQG.wait(eng, (spv, "v", t, half))
                        i = eng.tensor_copy(
                            v_tm[:, t * 1024 + half * 512:t * 1024 + (half + 1) * 512],
                            p_v[vix % 2][:])
                        sVE.inc(i, (vix,))
                        vix += 1
            else:
                for k in sorted(ladder):
                    ladder[k]()
            if tiles:
                _stats(tiles[2])
                _apply(tiles[1])
                _apply(tiles[2])
            sp = b - 1
            if 0 <= sp:
                sCS[sp % 2].wait(eng, ("cs", sp))
                t0p, ntp = SPANS[sp]
                qlo, qloc, qn = q_cols(sp)
                for g in range(4):
                    fc0 = g * 4
                    if fc0 < 8:
                        tl, n = qloc, qn
                        raw4 = q_fm[:].rearrange("p (c t) -> p c t", c=8)[
                            :, fc0:fc0 + 4, qlo * 128:qlo * 128 + n]
                    else:
                        tl, n = 0, ntp * 128
                        raw4 = k_fm[:].rearrange("p (c t) -> p c t", c=8)[
                            :, fc0 - 8:fc0 - 4, t0p * 128:t0p * 128 + n]
                    csb = crepS[:, tl:tl + n].unsqueeze(1).broadcast_to([128, 4, n])
                    ssb = srepS[:, tl:tl + n].unsqueeze(1).broadcast_to([128, 4, n])
                    pq4 = pq_sb[:].rearrange("p (s f) -> p s f", s=4)[:, :, 0:n]
                    m14 = m1b[:].rearrange("p (s f) -> p s f", s=4)[:, :, 0:n]
                    m24 = m2b[:].rearrange("p (s f) -> p s f", s=4)[:, :, 0:n]
                    sPQE.wait(eng, (sp, fc0 + 3))
                    sRP.selfwait(eng)
                    i = eng.tensor_tensor(m14, raw4, csb, ALU.mult)
                    sv.edge(i, eng)
                    i = eng.tensor_tensor(m24, pq4, ssb, ALU.mult)
                    sv.edge(i, eng)
                    i = eng.tensor_tensor(raw4, m14, m24, ALU.add)
                    sRP.inc(i, (sp, g))
        i = eng.nop()
        sAB.inc(i, ("v",))
        eng.wait_ge(sAB.h, 3)
        i = eng.memset(q_z[:], 0.0)
        sv.inc(i, ("qz",))
        sWL.wait(eng, (1,))
        i = eng.tensor_scalar_mul(wffn[:, 8 * 512:12 * 512],
                                  wffn[:, 4 * 512:8 * 512], -1.0)
        svW.inc(i, ("neg",))
        for qb in range(KNQL + 2):
            po = qb - 1
            so = qb - 2
            esl = (qb % 2) * 6144
            if 0 <= po < KNQL:
                sDR.wait(eng, (po,))
                i = eng.tensor_copy(rep_sb[:], p_scB[:])
                sv.edge(i, eng)
                sPA.wait(eng, (po,))
                sPJ.wait(eng, (po - 1,))
                eng.tensor_tensor(attn_fm[:, 0:512],
                                  p_att[:, 0:512], rep_sb[:, 0:512], ALU.mult)
                i = eng.tensor_tensor(attn_fm[:, 512:1024],
                                      p_att[:, 512:1024], rep_sb[:, 512:1024],
                                      ALU.mult)
                sAM.inc(i, (po,))
            if 0 <= po < KNQL:
                yo = (po % 2) * 1024
                sPJ.wait(eng, (po,))
                sl = 0 * 4 + (po % 2) * 2
                eng.bn_stats(bn6[:, 0:6], p_scB[:, 0:512])
                i = eng.bn_stats(bn6[:, 6:12], p_scB[:, 512:1024])
                sv.edge(i, eng)
                i = eng.bn_aggr(bnagB[:, sl:sl + 2], bn6[:, 0:12])
                sBN.inc(i, ("b", po, 0))
                sBN.selfwait(eng)
                sRS.wait(eng, ("b", po, 0))
                i = eng.tensor_scalar(negmr[:], bnagB[:, sl:sl + 1],
                                      rstdB[:, po % 2:po % 2 + 1], -1.0,
                                      ALU.mult, ALU.mult)
                sv.edge(i, eng)
                sXBp[po % 2].wait(eng, (po,))
                sOUTp[(po - 2) % 2].wait(eng, (po - 2,))
                i = eng.tensor_scalar(xs[:], xb[:, yo:yo + 1024], negmr[:],
                                      None, ALU.add)
                sv.edge(i, eng)
                eng.scalar_tensor_tensor(y1[:, yo:yo + 512], p_scB[:, 0:512],
                                         rstdB[:, po % 2:po % 2 + 1],
                                         xs[:, 0:512], ALU.mult, ALU.add)
                i = eng.scalar_tensor_tensor(y1[:, yo + 512:yo + 1024],
                                             p_scB[:, 512:1024],
                                             rstdB[:, po % 2:po % 2 + 1],
                                             xs[:, 512:1024], ALU.mult, ALU.add)
                sY1.inc(i, (po,))
                sY1.selfwait(eng)
            if 0 <= so < KNQL:
                sT2.wait(eng, (so,))
                sFF.wait(eng, (so - 1,))
                i = eng.tensor_copy(h2fm[:], p_tp2[:])
                sF2.inc(i, (so,))
            if 0 <= po < KNQL:
                yo = (po % 2) * 1024
                sl = 1 * 4 + (po % 2) * 2
                eng.bn_stats(bn6[:, 0:6], y1[:, yo:yo + 512])
                i = eng.bn_stats(bn6[:, 6:12], y1[:, yo + 512:yo + 1024])
                sv.edge(i, eng)
                i = eng.bn_aggr(bnagB[:, sl:sl + 2], bn6[:, 0:12])
                sBN.inc(i, ("b", po, 1))
                sBN.selfwait(eng)
                sRS.wait(eng, ("b", po, 1))
                sT2.wait(eng, (po - 1,))
                i = eng.tensor_scalar(h2[:], y1[:, yo:yo + 1024],
                                      bnagB[:, sl:sl + 1],
                                      rstdB[:, 2 + po % 2:2 + po % 2 + 1],
                                      ALU.subtract, ALU.mult)
                sH2.inc(i, (po,))
            if 0 <= so < KNQL:
                yo2 = (so % 2) * 1024
                sFF.wait(eng, (so,))
                sl = 2 * 4 + (so % 2) * 2
                eng.bn_stats(bn6[:, 12:18], p_smA[:])
                i = eng.bn_stats(bn6[:, 18:24], p_wb[:])
                sv.edge(i, eng)
                i = eng.bn_aggr(bnagB[:, sl:sl + 2], bn6[:, 12:24])
                sBN.inc(i, ("b", so, 2))
                sBN.selfwait(eng)
                sRS.wait(eng, ("b", so, 2))
                i = eng.tensor_scalar(negmr[:], bnagB[:, sl:sl + 1],
                                      rstdB[:, 4 + so % 2:4 + so % 2 + 1], -1.0,
                                      ALU.mult, ALU.mult)
                sv.edge(i, eng)
                sOUTp[(so - 1) % 2].wait(eng, (so - 1,))
                i = eng.tensor_scalar(xs[:], y1[:, yo2:yo2 + 1024], negmr[:],
                                      None, ALU.add)
                sv.edge(i, eng)
                eng.scalar_tensor_tensor(
                    xs[:, 0:512],
                    p_smA[:], rstdB[:, 4 + so % 2:4 + so % 2 + 1],
                    xs[:, 0:512], ALU.mult, ALU.add)
                i = eng.scalar_tensor_tensor(
                    xs[:, 512:1024],
                    p_wb[:], rstdB[:, 4 + so % 2:4 + so % 2 + 1],
                    xs[:, 512:1024], ALU.mult, ALU.add)
                sYO.inc(i, (so,))


    with ctx, nc.Block() as block:
        block.sync(em_sync)
        block.gpsimd(em_gpsimd)
        block.tensor(em_tensor)
        block.scalar(em_scalar)
        block.vector(em_vector)
    return nc


_CACHE = {}


def _get_nc():
    if "nc" not in _CACHE:
        ev = {}
        build_nc(ev)
        _CACHE["nc"] = build_nc(ev)
    return _CACHE["nc"]


def _host_constants():
    if "const" in _CACHE:
        return _CACHE["const"]
    idn = np.eye(128, dtype=np.float32).astype(BF)
    M = np.zeros((128, 128), np.float32)
    for blk in range(2):
        o = blk * 64
        for i in range(32):
            M[o + i, o + 32 + i] = -1.0
            M[o + 32 + i, o + i] = 1.0
    prot = np.ascontiguousarray(M.T).astype(BF)
    invf = 1.0 / (10000.0 ** (np.arange(0, 64, 2, dtype=np.float32) / 64.0))
    per_core = []
    for core in range(8):
        qt = core % 4
        lo = qt * TO - HW
        pos = np.clip(np.arange(lo, lo + TH), 0, 8191).astype(np.float32)
        fr = np.outer(pos, invf)
        crep = np.tile(np.ascontiguousarray(np.cos(fr).T), (4, 1)).astype(BF)
        srep = np.tile(np.ascontiguousarray(np.sin(fr).T), (4, 1)).astype(BF)
        j = np.arange(128)[:, None]
        r = np.arange(128)[None, :]
        msk = np.zeros((NQ, 2, 128, 128), np.float32)
        for qb in range(NQ):
            for mi, kb in ((0, 0), (1, 2)):
                band = (j >= r) if mi == 0 else (j <= r)
                gpos = lo + (qb + kb) * 128 + np.arange(128)
                ok = (gpos >= 0) & (gpos < 8192)
                msk[qb, mi] = band & ok[:, None]
        per_core.append((crep, srep, msk.astype(BF)))
    _CACHE["const"] = (idn, prot, per_core)
    return _CACHE["const"]


def kernel(L0_attn_x, L0_attn_Wqkv, L0_attn_Wout, L0_attn_bout,
           L0_ffn_G1_R, L0_ffn_G2_R, L0_ffn_G1_I, L0_ffn_G2_I,
           L0_ffn_bias_real, L0_ffn_bias_imag):
    x = np.ascontiguousarray(np.asarray(L0_attn_x, dtype=np.float32))
    idn, prot, per_core = _host_constants()
    shared = {
        "wqkv": np.ascontiguousarray(L0_attn_Wqkv, dtype=np.float32),
        "wout": np.ascontiguousarray(L0_attn_Wout, dtype=np.float32),
        "bout": np.ascontiguousarray(L0_attn_bout, dtype=np.float32),
        "g1r": np.ascontiguousarray(L0_ffn_G1_R, dtype=np.float32),
        "g2r": np.ascontiguousarray(L0_ffn_G2_R, dtype=np.float32),
        "g1i": np.ascontiguousarray(L0_ffn_G1_I, dtype=np.float32),
        "g2i": np.ascontiguousarray(L0_ffn_G2_I, dtype=np.float32),
        "br": np.ascontiguousarray(L0_ffn_bias_real, dtype=np.float32),
        "bi": np.ascontiguousarray(L0_ffn_bias_imag, dtype=np.float32),
        "idn": idn, "prot": prot,
    }
    in_maps = []
    for core in range(8):
        b, qt = core // 4, core % 4
        lo = qt * TO - HW
        xh = np.zeros((TH, D), np.float32)
        s0, s1 = max(lo, 0), min(lo + TH, 8192)
        xh[s0 - lo:s1 - lo] = x[b, s0:s1]
        crep, srep, msk = per_core[core]
        m = dict(shared)
        m.update({"xh": xh, "crep": crep, "srep": srep, "msk": msk})
        in_maps.append(m)
    nc = _get_nc()
    res = run_bass_kernel_spmd(nc, in_maps, core_ids=list(range(8)))
    out = np.empty((2, 8192, D), np.float32)
    for core in range(8):
        b, qt = core // 4, core % 4
        out[b, qt * TO:(qt + 1) * TO] = res.results[core]["y"]
    return out


# version 32 (final)
# speedup vs baseline: 1.1265x; 1.1140x over previous
import os
import numpy as np
import ml_dtypes
from contextlib import ExitStack

import concourse.bass as bass
import concourse.mybir as mybir
from concourse.bass_utils import run_bass_kernel_spmd

F32 = mybir.dt.float32
BF16 = mybir.dt.bfloat16
AF = mybir.ActivationFunctionType
ALU = mybir.AluOpType
BF = ml_dtypes.bfloat16

D = 1024
NH, HD = 16, 64
TO = 2048
HW = 128
TH = TO + 2 * HW
NT = TH // 128
NQ = TO // 128
EPS = 1e-5
KNQL = int(os.environ.get("KNQL", "16"))
KPHA = int(os.environ.get("KPHA", "1"))
KSEC = int(os.environ.get("KSEC", "8"))
SPANS = [(3 * i, 3) for i in range(6)]
NSP = len(SPANS)
TILES = [(kb, hg) for kb in (0, 2, 1) for hg in range(4)]


class Sem:
    def __init__(self, ctx, nc, name, ev):
        self.h = ctx.enter_context(nc.semaphore(name))
        self.n = 0
        self.name = name
        self.ev = ev

    def inc(self, inst, key=None, k=1):
        self.n += k
        if inst is not None and k > 0:
            inst.then_inc(self.h, k)
        if key is not None:
            self.ev[(self.name,) + tuple(key)] = self.n
        return self.n

    def wait(self, eng, key):
        t = self.ev.get((self.name,) + tuple(key), 0)
        if t > 0:
            eng.wait_ge(self.h, t)

    def selfwait(self, eng):
        eng.wait_ge(self.h, self.n)

    def edge(self, inst, eng):
        self.n += 1
        inst.then_inc(self.h, 1)
        eng.wait_ge(self.h, self.n)


def q_range(t0, nt):
    return max(t0 - 1, 0), min(t0 + nt - 1, NQ)


def q_cols(si):
    t0, nt = SPANS[si]
    qlo, qhi = q_range(t0, nt)
    return qlo, (qlo + 1 - t0) * 128, (qhi - qlo) * 128


def build_nc(ev):
    nc = bass.Bass("TRN2", target_bir_lowering=False)

    d_xh = nc.dram_tensor("xh", [TH, D], F32, kind="ExternalInput").ap()
    d_wqkv = nc.dram_tensor("wqkv", [D, 3 * D], F32, kind="ExternalInput").ap()
    d_wout = nc.dram_tensor("wout", [D, D], F32, kind="ExternalInput").ap()
    d_bout = nc.dram_tensor("bout", [D], F32, kind="ExternalInput").ap()
    d_g1 = [nc.dram_tensor("g1r", [1, 32, 4, 32], F32, kind="ExternalInput").ap(),
            nc.dram_tensor("g1i", [1, 32, 4, 32], F32, kind="ExternalInput").ap()]
    d_g2 = [nc.dram_tensor("g2r", [4, 16, 1, 16], F32, kind="ExternalInput").ap(),
            nc.dram_tensor("g2i", [4, 16, 1, 16], F32, kind="ExternalInput").ap()]
    d_br = nc.dram_tensor("br", [512], F32, kind="ExternalInput").ap()
    d_bi = nc.dram_tensor("bi", [512], F32, kind="ExternalInput").ap()
    d_idn = nc.dram_tensor("idn", [128, 128], BF16, kind="ExternalInput").ap()
    d_prot = nc.dram_tensor("prot", [128, 128], BF16, kind="ExternalInput").ap()
    d_crep = nc.dram_tensor("crep", [128, TH], BF16, kind="ExternalInput").ap()
    d_srep = nc.dram_tensor("srep", [128, TH], BF16, kind="ExternalInput").ap()
    d_msk = nc.dram_tensor("msk", [NQ, 2, 128, 128], BF16, kind="ExternalInput").ap()
    d_y = nc.dram_tensor("y", [TO, D], F32, kind="ExternalOutput").ap()
    d_scr = nc.dram_tensor("wscr", [2, 1024, 256], BF16, kind="Internal").ap()

    ctx = ExitStack()
    sbr = lambda n, s, d: ctx.enter_context(nc.sbuf_tensor(n, s, d))

    q_fm = sbr("q_fm", [128, 8 * TO], BF16)
    k_fm = sbr("k_fm", [128, 8 * TH], BF16)
    v_tm = sbr("v_tm", [128, NT * D], BF16)
    wffn = sbr("wffn", [128, 12 * 512], BF16)
    idn_sb = sbr("idn_sb", [128, 128], BF16)
    prot_sb = sbr("prot_sb", [128, 128], BF16)
    ones_k = sbr("ones_k", [128, 1], BF16)
    ones_r = sbr("ones_r", [1, 128], BF16)
    msk_sb = sbr("msk_sb", [128, 2 * 256], BF16)
    bn6 = sbr("bn6", [128, 2 * 12], F32)
    bnag = sbr("bnag", [128, 2 * 2], F32)
    lnv = sbr("lnv", [128, 1], F32)
    rstd = sbr("rstd", [128, 2], F32)
    eps_sb = sbr("eps_sb", [128, 1], F32)
    p_wb = ctx.enter_context(nc.psum_tensor("p_wb", [128, 512], F32))

    ctxA = ExitStack()
    sba = lambda n, s, d: ctxA.enter_context(nc.sbuf_tensor(n, s, d))
    psa = lambda n, s, d: ctxA.enter_context(nc.psum_tensor(n, s, d))
    wqkv_sb = sba("wqkv_sb", [128, 8 * 3072], BF16)
    s_g1 = [sba("s_g1r", [4, 1024], BF16), sba("s_g1i", [4, 1024], BF16)]
    s_g2 = [sba("s_g2r", [4, 256], BF16), sba("s_g2i", [4, 256], BF16)]
    s_g1db = sba("s_g1db", [4, 1024], BF16)
    s_g2ge = sba("s_g2ge", [4, 256], BF16)
    s_pw = sba("s_pw", [128, 2 * 256], BF16)
    crepS = sba("crepS", [128, 384], BF16)
    srepS = sba("srepS", [128, 384], BF16)
    xfm = sba("xfm", [128, 2 * 8 * 384], BF16)
    xa = sba("xa", [128, 2 * 1024], F32)
    xln = sba("xln", [128, 1024], BF16)
    m1b = sba("m1b", [128, 4 * 384], BF16)
    m2b = sba("m2b", [128, 4 * 384], BF16)
    pq_sb = sba("pq_sb", [128, 4 * 384], BF16)
    p_tp = [psa("p_tp0", [128, 1024], BF16)]
    p_qk = [psa("p_qk0", [128, 512], F32), psa("p_qk1", [128, 512], F32)]
    p_v = [psa("p_v0", [128, 512], F32), psa("p_v1", [128, 512], F32)]
    p_pq = [psa("p_pq0", [128, 512], F32), psa("p_pq1", [128, 512], F32)]
    ctxA.close()

    ctxB = ExitStack()
    sbb = lambda n, s, d: ctxB.enter_context(nc.sbuf_tensor(n, s, d))
    psb = lambda n, s, d: ctxB.enter_context(nc.psum_tensor(n, s, d))
    wout_sb = sbb("wout_sb", [128, 8 * 1024], BF16)
    bias_sb = sbb("bias_sb", [1, 2048], BF16)
    e_sb = sbb("e_sb", [128, 2 * 3 * 2048], BF16)
    xb = sbb("xb", [128, 2 * 1024], F32)
    y1 = sbb("y1", [128, 2 * 1024], F32)
    h2 = sbb("h2", [128, 1024], BF16)
    h2fm = sbb("h2fm", [128, 8 * 128], BF16)
    attn_fm = sbb("attn_fm", [128, 1024], BF16)
    rep_sb = sbb("rep_sb", [128, 1024], BF16)
    r_row = sbb("r_row", [1, 2048], BF16)
    rtmp33 = sbb("rtmp33", [33, 1024], F32)
    q_z = sbb("q_z", [128, 2 * 2048], BF16)
    xs = sbb("xs", [128, 1024], F32)
    bnagB = sbb("bnagB", [128, 3 * 2 * 2], F32)
    rstdB = sbb("rstdB", [128, 3 * 2], F32)
    lnvB = sbb("lnvB", [128, 1], F32)
    negmr = sbb("negmr", [128, 1], F32)
    p_scA = psb("p_scA", [128, 1024], F32)
    p_scB = psb("p_scB", [128, 1024], F32)
    p_att = psb("p_att", [128, 1024], F32)
    p_smA = psb("p_smA", [128, 512], F32)
    ctxB.close()

    p_tp2 = p_smA[:].bitcast(BF16)

    S = lambda n: Sem(ctx, nc, n, ev)
    sW = S("sW"); sCC = S("sCC")
    sCS = [S("sCS0"), S("sCS1")]
    sXp = [S("sX0"), S("sX1")]
    sBN = S("sBN"); sRS = S("sRS")
    sLN = S("sLN"); sTP = S("sTP"); sFM = S("sFM"); sQG = S("sQG")
    sQE = S("sQE"); sPQ = S("sPQ"); sPQE = S("sPQE"); sRP = S("sRP"); sVE = S("sVE")
    sAB = S("sAB"); sWM = S("sWM"); sWE = S("sWE")
    sWSp = [S("sWS0"), S("sWS1")]; sWL = S("sWL")
    sG = [S("sG0"), S("sG1")]; sBIA = S("sBIA"); sWO = S("sWO")
    sXBp = [S("sXB0"), S("sXB1")]; sMKp = [S("sMK0"), S("sMK1")]
    sQZp = [S("sQZ0"), S("sQZ1")]
    sSC = S("sSC"); sE = S("sE"); sEM = S("sEM")
    sSM = S("sSM"); sRC = S("sRC"); sRCln = S("sRCln"); sDR = S("sDR")
    sPA = S("sPA"); sAM = S("sAM"); sPJ = S("sPJ"); sY1 = S("sY1")
    sH2 = S("sH2"); sT2 = S("sT2"); sF2 = S("sF2"); sFF = S("sFF")
    sYO = S("sYO"); sOUTp = [S("sOUT0"), S("sOUT1")]
    sv = S("sv"); sa = S("sa"); svW = S("svW")

    def rope_slice(sp, fc):
        t0, nt = SPANS[sp]
        qlo, qloc, qn = q_cols(sp)
        if fc < 8:
            return (q_fm[:, fc * TO + qlo * 128:fc * TO + qlo * 128 + qn], qloc, qn)
        kc = fc - 8
        return (k_fm[:, kc * TH + t0 * 128:kc * TH + (t0 + nt) * 128], 0, nt * 128)

    def em_sync(eng):
        i = eng.dma_start(out=idn_sb[:], in_=d_idn)
        sCC.inc(i, None, 16)
        i = eng.dma_start(out=prot_sb[:], in_=d_prot)
        sCC.inc(i, ("cc",), 16)
        for si, (t0, nt) in enumerate(SPANS):
            for t in range(t0, t0 + nt):
                sLN.wait(eng, (t - 2,))
                i = eng.dma_start(out=xa[:, (t % 2) * 1024:(t % 2) * 1024 + 1024],
                                  in_=d_xh[t * 128:(t + 1) * 128, :])
                sXp[t % 2].inc(i, (t,), 16)
            sRP.wait(eng, (si - 1, 3))
            i = eng.dma_start(out=crepS[:],
                              in_=d_crep[:, t0 * 128:t0 * 128 + 384])
            sCS[si % 2].inc(i, None, 16)
            i = eng.dma_start(out=srepS[:],
                              in_=d_srep[:, t0 * 128:t0 * 128 + 384])
            sCS[si % 2].inc(i, ("cs", si), 16)
        eng.wait_ge(sAB.h, 3)
        for qb in range(KNQL + 2):
            if qb < KNQL:
                sSC.wait(eng, (qb - 2, 1, 3))
                if qb < 2:
                    sv.wait(eng, ("qz",))
                qcols = q_fm[:].rearrange("p (c t) -> p c t", c=8)[:, :, qb * 128:(qb + 1) * 128]
                zslot = (qb % 2) * 2048
                qzv = q_z[:, zslot:zslot + 2048].rearrange(
                    "p (c two t) -> p c two t", c=8, two=2)
                i = eng.dma_start(out=qzv[0:64, :, 0, :], in_=qcols[0:64])
                sQZp[qb % 2].inc(i, None, 16)
                i = eng.dma_start(out=qzv[64:128, :, 1, :], in_=qcols[64:128])
                sQZp[qb % 2].inc(i, (qb,), 16)
                sEM.wait(eng, (qb - 2, 2))
                i = eng.dma_start(
                    out=msk_sb[:, (qb % 2) * 256:(qb % 2) * 256 + 256]
                        .rearrange("p (mi f) -> p mi f", mi=2),
                    in_=d_msk[qb].rearrange("mi p f -> p mi f"))
                sMKp[qb % 2].inc(i, (qb,), 16)
                sY1.wait(eng, (qb - 2,))
                i = eng.dma_start(out=xb[:, (qb % 2) * 1024:(qb % 2) * 1024 + 1024],
                                  in_=d_xh[HW + qb * 128:HW + (qb + 1) * 128, :])
                sXBp[qb % 2].inc(i, (qb,), 16)

    def em_gpsimd(eng):
        for m in range(2):
            i = eng.dma_start(out=s_g1[m][:].rearrange("c (b d) -> c b d", b=32),
                              in_=d_g1[m][0].rearrange("b c d -> c b d"))
            sG[m].inc(i, None, 16)
            i = eng.dma_start(out=s_g2[m][:],
                              in_=d_g2[m][:, :, 0, :].rearrange("c e g -> c (e g)"))
            sG[m].inc(i, ("g", m), 16)
        for t3 in range(3):
            for dc in range(8):
                i = eng.dma_start(
                    out=wqkv_sb[:, dc * 3072 + t3 * 1024:
                                   dc * 3072 + (t3 + 1) * 1024],
                    in_=d_wqkv[dc * 128:(dc + 1) * 128,
                               t3 * 1024:(t3 + 1) * 1024])
                sW.inc(i, ("t3", t3) if dc == 7 else None, 16)
        for m in range(2):
            for mt in range(8):
                r = 8 * m + mt
                sWE.wait(eng, (m, mt))
                i = eng.dma_start(out=d_scr[m, mt * 128:(mt + 1) * 128, :],
                                  in_=s_pw[:, (r % 2) * 256:(r % 2) * 256 + 256])
                sWSp[r % 2].inc(i, (r,), 16)
            sWSp[0].wait(eng, (8 * m + 6,))
            sWSp[1].wait(eng, (8 * m + 7,))
            scr_r = d_scr[m].rearrange("(d b) (g e) -> d g b e", d=32, b=32, g=16)
            base = 0 if m == 0 else 4
            for ci in range(4):
                for dr in range(8):
                    i = eng.dma_start(
                        out=wffn[dr * 16:(dr + 1) * 16,
                                 (base + ci) * 512:(base + ci + 1) * 512],
                        in_=scr_r[ci * 8 + dr])
                    sWL.inc(i, None, 16)
            sWL.inc(None, (m,), 0)
        eng.wait_ge(sAB.h, 3)
        i = eng.dma_start(out=bias_sb[0:1, 0:1024], in_=d_bout.unsqueeze(0))
        sBIA.inc(i, None, 16)
        i = eng.dma_start(out=bias_sb[0:1, 1024:1536], in_=d_br.unsqueeze(0))
        sBIA.inc(i, None, 16)
        i = eng.dma_start(out=bias_sb[0:1, 1536:2048], in_=d_bi.unsqueeze(0))
        sBIA.inc(i, ("bias",), 16)
        for dc in range(8):
            i = eng.dma_start(out=wout_sb[:, dc * 1024:(dc + 1) * 1024],
                              in_=d_wout[dc * 128:(dc + 1) * 128, :])
            sWO.inc(i, ("wout",) if dc == 7 else None, 16)
        for qb in range(KNQL + 2):
            so = qb - 2
            if qb < KNQL:
                esl = (qb % 2) * 6144
                for hg in range(4):
                    sE.wait(eng, (qb, 0, hg))
                sMKp[qb % 2].wait(eng, (qb,))
                mk0 = msk_sb[:, (qb % 2) * 256:(qb % 2) * 256 + 128]
                ev0 = e_sb[:, esl:esl + 2048].rearrange("p (h f) -> p h f", h=16)
                i = eng.tensor_tensor(
                    ev0, ev0, mk0.unsqueeze(1).broadcast_to([128, 16, 128]),
                    ALU.mult)
                sEM.inc(i, (qb, 0))
                for hg in range(4):
                    sE.wait(eng, (qb, 2, hg))
                mk2 = msk_sb[:, (qb % 2) * 256 + 128:(qb % 2) * 256 + 256]
                ev2 = e_sb[:, esl + 4096:esl + 6144].rearrange("p (h f) -> p h f", h=16)
                i = eng.tensor_tensor(
                    ev2, ev2, mk2.unsqueeze(1).broadcast_to([128, 16, 128]),
                    ALU.mult)
                sEM.inc(i, (qb, 2))
            if 0 <= so < KNQL:
                sYO.wait(eng, (so,))
                i = eng.dma_start(out=d_y[so * 128:(so + 1) * 128, :], in_=xs[:])
                sOUTp[so % 2].inc(i, (so,), 16)
        for sp_ in sOUTp:
            if sp_.n > 0:
                eng.wait_ge(sp_.h, sp_.n)

    def em_tensor(eng):
        wrounds = [(m, mt) for m in range(2) for mt in range(8)]
        WSL = [(0, 3), (3, 6), (6, 8), (8, 11), (11, 14), (14, 16), (16, 16)]

        def _wmm(b):
            for m, mt in wrounds[WSL[b][0]:WSL[b][1]]:
                if mt == 0:
                    svW.wait(eng, ("sh", m))
                if mt >= 1:
                    sWE.wait(eng, (m, mt - 1))
                elif m == 1:
                    sWE.wait(eng, (0, 7))
                i = eng.matmul(p_wb[:, 0:256], s_g1db[:, mt * 128:(mt + 1) * 128],
                               s_g2ge[:], start=True, stop=True)
                sWM.inc(i, (m, mt))
        sCC.wait(eng, ("cc",))
        vix = 0
        for b in range(NSP + 1):
            _wmm(b)
            if b < NSP:
                si = b
                t0, nt = SPANS[si]
                for t in range(t0, t0 + nt):
                    sLN.wait(eng, (t,))
                    sFM.wait(eng, (t - 1,))
                    for c in range(8):
                        i = eng.transpose(p_tp[0][:, c * 128:(c + 1) * 128],
                                          xln[:, c * 128:(c + 1) * 128], idn_sb[:])
                    sTP.inc(i, (t,))
                sFM.wait(eng, (t0 + nt - 1,))
                qlo, qloc, qn = q_cols(si)
                for fc in range(16):
                    if si == 0 and fc in (0, 8):
                        sW.wait(eng, ("t3", fc // 8))
                    bank = p_qk[fc % 2]
                    if fc >= 2:
                        sQE.wait(eng, (si, fc - 2))
                    else:
                        sQE.wait(eng, (si - 1, 14 + fc))
                    off, n = (qloc, qn) if fc < 8 else (0, nt * 128)
                    for dc in range(8):
                        i = eng.matmul(
                            bank[:, 0:n],
                            wqkv_sb[:, dc * 3072 + fc * 128:dc * 3072 + (fc + 1) * 128],
                            xfm[:, (si % 2) * 3072 + dc * 384 + off:
                                   (si % 2) * 3072 + dc * 384 + off + n],
                            start=(dc == 0), stop=(dc == 7))
                    sQG.inc(i, (si, fc))
                    if si >= 1:
                        sp = si - 1
                        raw, _, pn = rope_slice(sp, fc)
                        sQE.wait(eng, (sp, fc))
                        sPQE.wait(eng, (sp, fc - 2))
                        if fc < 2:
                            sPQE.wait(eng, (sp - 1, 14 + fc))
                        i = eng.matmul(p_pq[fc % 2][:, 0:pn], prot_sb[:], raw,
                                       start=True, stop=True)
                        sPQ.inc(i, (sp, fc))
                for t in range(t0, t0 + nt):
                    for half in range(2):
                        if si == 0 and t == t0 and half == 0:
                            sW.wait(eng, ("t3", 2))
                        bank = p_v[vix % 2]
                        sVE.wait(eng, (vix - 2,))
                        for dc in range(8):
                            i = eng.matmul(
                                bank[:],
                                xfm[:, (si % 2) * 3072 + dc * 384 + (t - t0) * 128:
                                       (si % 2) * 3072 + dc * 384 + (t - t0) * 128 + 128],
                                wqkv_sb[:, dc * 3072 + 2048 + half * 512:
                                           dc * 3072 + 2048 + (half + 1) * 512],
                                start=(dc == 0), stop=(dc == 7))
                        sQG.inc(i, (si, "v", t, half))
                        vix += 1
                sQG.inc(None, (si, "vend"), 0)
            if b == NSP:
                sp = NSP - 1
                for fc in range(16):
                    raw, _, pn = rope_slice(sp, fc)
                    sQE.wait(eng, (sp, fc))
                    sPQE.wait(eng, (sp, fc - 2))
                    if fc < 2:
                        sPQE.wait(eng, (sp - 1, 14 + fc))
                    i = eng.matmul(p_pq[fc % 2][:, 0:pn], prot_sb[:], raw,
                                   start=True, stop=True)
                    sPQ.inc(i, (sp, fc))
        i = eng.nop()
        sAB.inc(i, ("t",))
        eng.wait_ge(sAB.h, 3)
        for qb in range(KNQL + 2):
            po = qb - 1
            so = qb - 2

            def _sc(lo, hi):
                if qb >= KNQL:
                    return
                if qb < 2 and lo == 0:
                    sv.wait(eng, ("qz",))
                for ti in range(lo, hi):
                    kb, hg = TILES[ti]
                    bank = p_scA[:, (ti % 2) * 512:(ti % 2) * 512 + 512]
                    if ti >= 2:
                        pkb, phg = TILES[ti - 2]
                        sE.wait(eng, (qb, pkb, phg))
                    else:
                        pkb, phg = TILES[10 + ti]
                        sE.wait(eng, (qb - 1, pkb, phg))
                        sRCln.wait(eng, (qb - 1,))
                    if ti == 0:
                        sQZp[qb % 2].wait(eng, (qb,))
                    for cc in range(2):
                        c = hg * 2 + cc
                        i = eng.matmul(
                            bank[:, cc * 256:(cc + 1) * 256],
                            k_fm[:, c * TH + (qb + kb) * 128:
                                    c * TH + (qb + kb) * 128 + 128],
                            q_z[:, (qb % 2) * 2048 + c * 256:
                                   (qb % 2) * 2048 + (c + 1) * 256],
                            start=True, stop=True)
                    sSC.inc(i, (qb, kb, hg))

            if 0 <= po < KNQL:
                sEM.wait(eng, (po, 0))
                sEM.wait(eng, (po, 2))
                for hg in range(4):
                    sE.wait(eng, (po, 1, hg))
                sAM.wait(eng, (po - 1,))
                esl = (po % 2) * 6144
                for c in range(8):
                    for par in range(2):
                        h = 2 * c + par
                        for kb in range(3):
                            i = eng.matmul(
                                p_att[par * 64:(par + 1) * 64, c * 128:(c + 1) * 128],
                                v_tm[:, (po + kb) * 1024 + h * 64:
                                        (po + kb) * 1024 + (h + 1) * 64],
                                e_sb[:, esl + kb * 2048 + h * 128:
                                        esl + kb * 2048 + (h + 1) * 128],
                                start=(kb == 0), stop=(kb == 2),
                                tile_position=(0, 64) if par else None)
                sPA.inc(i, (po,))
            _sc(0, 2)
            if 0 <= po < KNQL:
                sRC.wait(eng, (po,))
                sY1.wait(eng, (po - 1,))
                for c in range(8):
                    for par in range(2):
                        h = 2 * c + par
                        i = eng.matmul(
                            p_scB[par * 64:(par + 1) * 64, c * 128:(c + 1) * 128],
                            ones_r[0:1, 0:64],
                            r_row[0:1, h * 128:(h + 1) * 128],
                            start=True, stop=True,
                            tile_position=(0, 64) if par else None)
                sDR.inc(i, (po,))
            _sc(2, 6)
            if 0 <= po < KNQL:
                sAM.wait(eng, (po,))
                sY1.wait(eng, (po - 1,))
                if po == 0:
                    sWO.wait(eng, ("wout",))
                    sBIA.wait(eng, ("bias",))
                    sWE.wait(eng, (1, 7))
                for half in range(2):
                    for c in range(8):
                        eng.matmul(
                            p_scB[:, half * 512:(half + 1) * 512],
                            attn_fm[:, c * 128:(c + 1) * 128],
                            wout_sb[:, c * 1024 + half * 512:c * 1024 + (half + 1) * 512],
                            start=(c == 0), stop=False)
                    i = eng.matmul(p_scB[:, half * 512:(half + 1) * 512],
                                   ones_r[0:1, :],
                                   bias_sb[0:1, half * 512:(half + 1) * 512],
                                   start=False, stop=True)
                sPJ.inc(i, (po,))
            _sc(6, 10)
            if 0 <= so < KNQL:
                sH2.wait(eng, (so,))
                sYO.wait(eng, (so - 1,))
                for c in range(8):
                    i = eng.transpose(p_tp2[:, c * 128:(c + 1) * 128],
                                      h2[:, c * 128:(c + 1) * 128],
                                      idn_sb[:])
                sT2.inc(i, (so,))
            _sc(10, 12)
            if 0 <= so < KNQL:
                sF2.wait(eng, (so,))
                if so == 0:
                    svW.wait(eng, ("neg",))
                for half in range(2):
                    tgt = p_smA[:] if half == 0 else p_wb[:]
                    slots = ([0, 8] if half == 0 else [4, 0])
                    for ci in range(4):
                        eng.matmul(
                            tgt,
                            h2fm[:, ci * 128:(ci + 1) * 128],
                            wffn[:, (slots[0] + ci) * 512:(slots[0] + ci + 1) * 512],
                            start=(ci == 0), stop=False)
                    for ci in range(4):
                        eng.matmul(
                            tgt,
                            h2fm[:, (4 + ci) * 128:(4 + ci + 1) * 128],
                            wffn[:, (slots[1] + ci) * 512:(slots[1] + ci + 1) * 512],
                            start=False, stop=False)
                    i = eng.matmul(tgt,
                                   ones_r[0:1, :],
                                   bias_sb[0:1, 1024 + half * 512:1024 + (half + 1) * 512],
                                   start=False, stop=True)
                sFF.inc(i, (so,))
            if qb < KNQL:
                sE.wait(eng, (qb, TILES[10][0], TILES[10][1]))
                sE.wait(eng, (qb, TILES[11][0], TILES[11][1]))
                sEM.wait(eng, (qb, 0))
                sEM.wait(eng, (qb, 2))
                for hg in range(4):
                    sE.wait(eng, (qb, 1, hg))
                esl2 = (qb % 2) * 6144
                for g in range(4):
                    pg = (g % 2) * 32
                    for kb in range(3):
                        i = eng.matmul(
                            p_scA[pg:pg + 1, (g // 2) * 512:(g // 2) * 512 + 512],
                            ones_k[:],
                            e_sb[:, esl2 + kb * 2048 + g * 512:
                                    esl2 + kb * 2048 + (g + 1) * 512],
                            start=(kb == 0), stop=(kb == 2))
                sSM.inc(i, (qb,))

    def em_scalar(eng):
        sv.wait(eng, ("init",))
        wrounds = [(m, mt) for m in range(2) for mt in range(8)]
        WSL = [(0, 3), (3, 6), (6, 8), (8, 11), (11, 14), (14, 16), (16, 16)]

        def _wevac(b):
            for m, mt in wrounds[WSL[b][0]:WSL[b][1]]:
                r = 8 * m + mt
                sWM.wait(eng, (m, mt))
                if r >= 2:
                    sWSp[r % 2].wait(eng, (r - 2,))
                i = eng.copy(s_pw[:, (r % 2) * 256:(r % 2) * 256 + 256],
                             p_wb[:, 0:256])
                sWE.inc(i, (m, mt))
        def _rstd(b, t):
            sLN.wait(eng, (t - 2,))
            sBN.wait(eng, (t,))
            sl = t % 2
            i = eng.activation(lnv[:], bnag[:, 2 * sl + 1:2 * sl + 2],
                               AF.Ln, bias=eps_sb[:], scale=1.0)
            sa.edge(i, eng)
            i = eng.activation(rstd[:, sl:sl + 1], lnv[:], AF.Exp,
                               bias=0.0, scale=-0.5)
            sRS.inc(i, (t,))
        for b in range(NSP + 1):
            if b < NSP:
                t0, nt = SPANS[b]
                _rstd(b, t0)
                _rstd(b, t0 + 1)
            _wevac(b)
            if b < NSP:
                _rstd(b, t0 + 2)
                for t in range(t0, t0 + nt):
                    sTP.wait(eng, (t,))
                    sQG.wait(eng, (b - 2, "vend"))
                    i = eng.copy(
                        xfm[:, (b % 2) * 3072:(b % 2) * 3072 + 3072]
                            .rearrange("p (c f) -> p c f", c=8)
                            [:, :, (t - t0) * 128:(t - t0) * 128 + 128],
                        p_tp[0][:].rearrange("p (c f) -> p c f", c=8))
                    sFM.inc(i, (t,))
            for fc in range(16):
                if b < NSP:
                    raw, _, n = rope_slice(b, fc)
                    sQG.wait(eng, (b, fc))
                    i = eng.copy(raw, p_qk[fc % 2][:, 0:n])
                    sQE.inc(i, (b, fc))
                if b >= 1:
                    sp = b - 1
                    _, _, n = rope_slice(sp, fc)
                    sPQ.wait(eng, (sp, fc))
                    if fc >= 4:
                        sRP.wait(eng, (sp, fc // 4 - 1))
                    i = eng.copy(pq_sb[:, (fc % 4) * 384:(fc % 4) * 384 + n],
                                 p_pq[fc % 2][:, 0:n])
                    sPQE.inc(i, (sp, fc))
        i = eng.nop()
        sAB.inc(i, ("s",))
        eng.wait_ge(sAB.h, 3)
        def _rstd_b(x, li):
            sBN.wait(eng, ("b", x, li))
            sl = li * 4 + (x % 2) * 2
            i = eng.activation(lnvB[:], bnagB[:, sl + 1:sl + 2],
                               AF.Ln, bias=eps_sb[:], scale=1.0)
            sa.edge(i, eng)
            i = eng.activation(rstdB[:, li * 2 + x % 2:li * 2 + x % 2 + 1],
                               lnvB[:], AF.Exp, bias=0.0, scale=-0.5)
            sRS.inc(i, ("b", x, li))

        for qb in range(KNQL + 2):
            po = qb - 1
            so = qb - 2
            esl = (qb % 2) * 6144

            def _exp(lo, hi):
                if qb >= KNQL:
                    return
                for ti in range(lo, hi):
                    kb, hg = TILES[ti]
                    sSC.wait(eng, (qb, kb, hg))
                    if ti == 0:
                        sPA.wait(eng, (qb - 2,))
                    i = eng.activation(
                        e_sb[:, esl + kb * 2048 + hg * 512:
                                esl + kb * 2048 + (hg + 1) * 512],
                        p_scA[:, (ti % 2) * 512:(ti % 2) * 512 + 512],
                        AF.Exp, bias=0.0, scale=0.125)
                    sE.inc(i, (qb, kb, hg))

            _exp(0, 10)
            if 0 <= po < KNQL:
                _rstd_b(po, 0)
            if 0 <= so < KNQL:
                sT2.wait(eng, (so,))
                sFF.wait(eng, (so - 1,))
                i = eng.copy(h2fm[:], p_tp2[:])
                sF2.inc(i, (so,))
            _exp(10, 12)
            if 0 <= po < KNQL:
                _rstd_b(po, 1)
            if qb < KNQL:
                sSM.wait(eng, (qb,))
                i = eng.activation(rtmp33[:], p_scA[0:33, 0:1024],
                                   AF.Ln, bias=0.0, scale=1.0)
                sRCln.inc(i, (qb,))
                sRCln.selfwait(eng)
                rrv = r_row[0:1, :].rearrange("o (a two b) -> o a two b",
                                              a=2, two=2, b=512)
                for pp in range(2):
                    pg = pp * 32
                    i = eng.activation(rrv[:, :, pp, :],
                                       rtmp33[pg:pg + 1, :].rearrange(
                                           "o (a b) -> o a b", a=2),
                                       AF.Exp, bias=0.0, scale=-1.0)
                sRC.inc(i, (qb,))
            if 0 <= so < KNQL:
                _rstd_b(so, 2)

    def em_vector(eng):
        eng.memset(ones_k[:], 1.0)
        eng.memset(ones_r[:], 1.0)
        i = eng.memset(eps_sb[:], EPS)
        sv.inc(i, ("init",))
        def _wshuf(m):
            sG[m].wait(eng, ("g", m))
            if m == 1:
                sWM.wait(eng, (0, 7))
            eng.tensor_copy(s_g1db[:].rearrange("c (d b) -> c d b", d=32),
                            s_g1[m][:].rearrange("c (b d) -> c d b", b=32))
            i = eng.tensor_copy(s_g2ge[:].rearrange("c (g e) -> c g e", g=16),
                                s_g2[m][:].rearrange("c (e g) -> c g e", e=16))
            svW.inc(i, ("sh", m))
        _wshuf(0)
        vix = 0
        def _stats(t):
            sXp[t % 2].wait(eng, (t,))
            sRS.wait(eng, (t - 2,))
            sl = t % 2
            sLN.selfwait(eng)
            eng.bn_stats(bn6[:, sl * 12:sl * 12 + 6],
                         xa[:, sl * 1024:sl * 1024 + 512])
            i = eng.bn_stats(bn6[:, sl * 12 + 6:sl * 12 + 12],
                             xa[:, sl * 1024 + 512:sl * 1024 + 1024])
            sv.edge(i, eng)
            i = eng.bn_aggr(bnag[:, sl * 2:sl * 2 + 2],
                            bn6[:, sl * 12:sl * 12 + 12])
            sBN.inc(i, (t,))
        def _apply(t):
            sl = t % 2
            sBN.selfwait(eng)
            sRS.wait(eng, (t,))
            sTP.wait(eng, (t - 1,))
            i = eng.tensor_scalar(xln[:], xa[:, sl * 1024:sl * 1024 + 1024],
                                  bnag[:, sl * 2:sl * 2 + 1],
                                  rstd[:, sl:sl + 1], ALU.subtract, ALU.mult)
            sLN.inc(i, (t,))
        for b in range(NSP + 1):
            if b == 3:
                _wshuf(1)
            tiles = (list(range(SPANS[b][0], SPANS[b][0] + SPANS[b][1]))
                     if b < NSP else [])
            ladder = {0: (lambda: _stats(tiles[0])),
                      1: (lambda: _stats(tiles[1])),
                      2: (lambda: _apply(tiles[0]))} if tiles else {}
            if 1 <= b:
                spv = b - 1
                t0p, ntp = SPANS[spv]
                k = 0
                for t in range(t0p, t0p + ntp):
                    for half in range(2):
                        if k in ladder:
                            ladder[k]()
                        k += 1
                        sQG.wait(eng, (spv, "v", t, half))
                        i = eng.tensor_copy(
                            v_tm[:, t * 1024 + half * 512:t * 1024 + (half + 1) * 512],
                            p_v[vix % 2][:])
                        sVE.inc(i, (vix,))
                        vix += 1
            else:
                for k in sorted(ladder):
                    ladder[k]()
            if tiles:
                _stats(tiles[2])
                _apply(tiles[1])
                _apply(tiles[2])
            sp = b - 1
            if 0 <= sp:
                sCS[sp % 2].wait(eng, ("cs", sp))
                t0p, ntp = SPANS[sp]
                qlo, qloc, qn = q_cols(sp)
                for g in range(4):
                    fc0 = g * 4
                    if fc0 < 8:
                        tl, n = qloc, qn
                        raw4 = q_fm[:].rearrange("p (c t) -> p c t", c=8)[
                            :, fc0:fc0 + 4, qlo * 128:qlo * 128 + n]
                    else:
                        tl, n = 0, ntp * 128
                        raw4 = k_fm[:].rearrange("p (c t) -> p c t", c=8)[
                            :, fc0 - 8:fc0 - 4, t0p * 128:t0p * 128 + n]
                    csb = crepS[:, tl:tl + n].unsqueeze(1).broadcast_to([128, 4, n])
                    ssb = srepS[:, tl:tl + n].unsqueeze(1).broadcast_to([128, 4, n])
                    pq4 = pq_sb[:].rearrange("p (s f) -> p s f", s=4)[:, :, 0:n]
                    m14 = m1b[:].rearrange("p (s f) -> p s f", s=4)[:, :, 0:n]
                    m24 = m2b[:].rearrange("p (s f) -> p s f", s=4)[:, :, 0:n]
                    sPQE.wait(eng, (sp, fc0 + 3))
                    sRP.selfwait(eng)
                    i = eng.tensor_tensor(m14, raw4, csb, ALU.mult)
                    sv.edge(i, eng)
                    i = eng.tensor_tensor(m24, pq4, ssb, ALU.mult)
                    sv.edge(i, eng)
                    i = eng.tensor_tensor(raw4, m14, m24, ALU.add)
                    sRP.inc(i, (sp, g))
        i = eng.nop()
        sAB.inc(i, ("v",))
        eng.wait_ge(sAB.h, 3)
        i = eng.memset(q_z[:], 0.0)
        sv.inc(i, ("qz",))
        sWL.wait(eng, (1,))
        i = eng.tensor_scalar_mul(wffn[:, 8 * 512:12 * 512],
                                  wffn[:, 4 * 512:8 * 512], -1.0)
        svW.inc(i, ("neg",))
        for qb in range(KNQL + 2):
            po = qb - 1
            so = qb - 2
            esl = (qb % 2) * 6144
            if 0 <= po < KNQL:
                sDR.wait(eng, (po,))
                i = eng.tensor_copy(rep_sb[:], p_scB[:])
                sv.edge(i, eng)
                sPA.wait(eng, (po,))
                sPJ.wait(eng, (po - 1,))
                eng.tensor_tensor(attn_fm[:, 0:512],
                                  p_att[:, 0:512], rep_sb[:, 0:512], ALU.mult)
                i = eng.tensor_tensor(attn_fm[:, 512:1024],
                                      p_att[:, 512:1024], rep_sb[:, 512:1024],
                                      ALU.mult)
                sAM.inc(i, (po,))
            if 0 <= po < KNQL:
                yo = (po % 2) * 1024
                sPJ.wait(eng, (po,))
                sl = 0 * 4 + (po % 2) * 2
                eng.bn_stats(bn6[:, 0:6], p_scB[:, 0:512])
                i = eng.bn_stats(bn6[:, 6:12], p_scB[:, 512:1024])
                sv.edge(i, eng)
                i = eng.bn_aggr(bnagB[:, sl:sl + 2], bn6[:, 0:12])
                sBN.inc(i, ("b", po, 0))
                sBN.selfwait(eng)
                sRS.wait(eng, ("b", po, 0))
                i = eng.tensor_scalar(negmr[:], bnagB[:, sl:sl + 1],
                                      rstdB[:, po % 2:po % 2 + 1], -1.0,
                                      ALU.mult, ALU.mult)
                sv.edge(i, eng)
                sXBp[po % 2].wait(eng, (po,))
                sOUTp[(po - 2) % 2].wait(eng, (po - 2,))
                i = eng.tensor_scalar(xs[:], xb[:, yo:yo + 1024], negmr[:],
                                      None, ALU.add)
                sv.edge(i, eng)
                eng.scalar_tensor_tensor(y1[:, yo:yo + 512], p_scB[:, 0:512],
                                         rstdB[:, po % 2:po % 2 + 1],
                                         xs[:, 0:512], ALU.mult, ALU.add)
                i = eng.scalar_tensor_tensor(y1[:, yo + 512:yo + 1024],
                                             p_scB[:, 512:1024],
                                             rstdB[:, po % 2:po % 2 + 1],
                                             xs[:, 512:1024], ALU.mult, ALU.add)
                sY1.inc(i, (po,))
                sY1.selfwait(eng)
            if 0 <= po < KNQL:
                yo = (po % 2) * 1024
                sl = 1 * 4 + (po % 2) * 2
                eng.bn_stats(bn6[:, 0:6], y1[:, yo:yo + 512])
                i = eng.bn_stats(bn6[:, 6:12], y1[:, yo + 512:yo + 1024])
                sv.edge(i, eng)
                i = eng.bn_aggr(bnagB[:, sl:sl + 2], bn6[:, 0:12])
                sBN.inc(i, ("b", po, 1))
                sBN.selfwait(eng)
                sRS.wait(eng, ("b", po, 1))
                sT2.wait(eng, (po - 1,))
                i = eng.tensor_scalar(h2[:], y1[:, yo:yo + 1024],
                                      bnagB[:, sl:sl + 1],
                                      rstdB[:, 2 + po % 2:2 + po % 2 + 1],
                                      ALU.subtract, ALU.mult)
                sH2.inc(i, (po,))
            if 0 <= so < KNQL:
                yo2 = (so % 2) * 1024
                sFF.wait(eng, (so,))
                sl = 2 * 4 + (so % 2) * 2
                eng.bn_stats(bn6[:, 12:18], p_smA[:])
                i = eng.bn_stats(bn6[:, 18:24], p_wb[:])
                sv.edge(i, eng)
                i = eng.bn_aggr(bnagB[:, sl:sl + 2], bn6[:, 12:24])
                sBN.inc(i, ("b", so, 2))
                sBN.selfwait(eng)
                sRS.wait(eng, ("b", so, 2))
                i = eng.tensor_scalar(negmr[:], bnagB[:, sl:sl + 1],
                                      rstdB[:, 4 + so % 2:4 + so % 2 + 1], -1.0,
                                      ALU.mult, ALU.mult)
                sv.edge(i, eng)
                sOUTp[(so - 1) % 2].wait(eng, (so - 1,))
                i = eng.tensor_scalar(xs[:], y1[:, yo2:yo2 + 1024], negmr[:],
                                      None, ALU.add)
                sv.edge(i, eng)
                eng.scalar_tensor_tensor(
                    xs[:, 0:512],
                    p_smA[:], rstdB[:, 4 + so % 2:4 + so % 2 + 1],
                    xs[:, 0:512], ALU.mult, ALU.add)
                i = eng.scalar_tensor_tensor(
                    xs[:, 512:1024],
                    p_wb[:], rstdB[:, 4 + so % 2:4 + so % 2 + 1],
                    xs[:, 512:1024], ALU.mult, ALU.add)
                sYO.inc(i, (so,))


    with ctx, nc.Block() as block:
        block.sync(em_sync)
        block.gpsimd(em_gpsimd)
        block.tensor(em_tensor)
        block.scalar(em_scalar)
        block.vector(em_vector)
    return nc


_CACHE = {}


def _get_nc():
    if "nc" not in _CACHE:
        ev = {}
        build_nc(ev)
        _CACHE["nc"] = build_nc(ev)
    return _CACHE["nc"]


def _host_constants():
    if "const" in _CACHE:
        return _CACHE["const"]
    idn = np.eye(128, dtype=np.float32).astype(BF)
    M = np.zeros((128, 128), np.float32)
    for blk in range(2):
        o = blk * 64
        for i in range(32):
            M[o + i, o + 32 + i] = -1.0
            M[o + 32 + i, o + i] = 1.0
    prot = np.ascontiguousarray(M.T).astype(BF)
    invf = 1.0 / (10000.0 ** (np.arange(0, 64, 2, dtype=np.float32) / 64.0))
    per_core = []
    for core in range(8):
        qt = core % 4
        lo = qt * TO - HW
        pos = np.clip(np.arange(lo, lo + TH), 0, 8191).astype(np.float32)
        fr = np.outer(pos, invf)
        crep = np.tile(np.ascontiguousarray(np.cos(fr).T), (4, 1)).astype(BF)
        srep = np.tile(np.ascontiguousarray(np.sin(fr).T), (4, 1)).astype(BF)
        j = np.arange(128)[:, None]
        r = np.arange(128)[None, :]
        msk = np.zeros((NQ, 2, 128, 128), np.float32)
        for qb in range(NQ):
            for mi, kb in ((0, 0), (1, 2)):
                band = (j >= r) if mi == 0 else (j <= r)
                gpos = lo + (qb + kb) * 128 + np.arange(128)
                ok = (gpos >= 0) & (gpos < 8192)
                msk[qb, mi] = band & ok[:, None]
        per_core.append((crep, srep, msk.astype(BF)))
    _CACHE["const"] = (idn, prot, per_core)
    return _CACHE["const"]


def kernel(L0_attn_x, L0_attn_Wqkv, L0_attn_Wout, L0_attn_bout,
           L0_ffn_G1_R, L0_ffn_G2_R, L0_ffn_G1_I, L0_ffn_G2_I,
           L0_ffn_bias_real, L0_ffn_bias_imag):
    x = np.ascontiguousarray(np.asarray(L0_attn_x, dtype=np.float32))
    idn, prot, per_core = _host_constants()
    shared = {
        "wqkv": np.ascontiguousarray(L0_attn_Wqkv, dtype=np.float32),
        "wout": np.ascontiguousarray(L0_attn_Wout, dtype=np.float32),
        "bout": np.ascontiguousarray(L0_attn_bout, dtype=np.float32),
        "g1r": np.ascontiguousarray(L0_ffn_G1_R, dtype=np.float32),
        "g2r": np.ascontiguousarray(L0_ffn_G2_R, dtype=np.float32),
        "g1i": np.ascontiguousarray(L0_ffn_G1_I, dtype=np.float32),
        "g2i": np.ascontiguousarray(L0_ffn_G2_I, dtype=np.float32),
        "br": np.ascontiguousarray(L0_ffn_bias_real, dtype=np.float32),
        "bi": np.ascontiguousarray(L0_ffn_bias_imag, dtype=np.float32),
        "idn": idn, "prot": prot,
    }
    in_maps = []
    for core in range(8):
        b, qt = core // 4, core % 4
        lo = qt * TO - HW
        xh = np.zeros((TH, D), np.float32)
        s0, s1 = max(lo, 0), min(lo + TH, 8192)
        xh[s0 - lo:s1 - lo] = x[b, s0:s1]
        crep, srep, msk = per_core[core]
        m = dict(shared)
        m.update({"xh": xh, "crep": crep, "srep": srep, "msk": msk})
        in_maps.append(m)
    nc = _get_nc()
    res = run_bass_kernel_spmd(nc, in_maps, core_ids=list(range(8)))
    out = np.empty((2, 8192, D), np.float32)
    for core in range(8):
        b, qt = core // 4, core % 4
        out[b, qt * TO:(qt + 1) * TO] = res.results[core]["y"]
    return out
